# version 15
# baseline (speedup 1.0000x reference)
import os
import numpy as np
import concourse.bass as bass
import concourse.mybir as mybir
from concourse.bass_utils import run_bass_kernel_spmd

F32 = mybir.dt.float32
BF16 = mybir.dt.bfloat16
AF = mybir.ActivationFunctionType
ALU = mybir.AluOpType
AX = mybir.AxisListType

TOK = 2048
D = 1024
C = 128
E = 16
FP = 256
NT = TOK // 128
DC = D // 128
NLEV = 4
LNEPS = 1e-5


class KB:
    ENG = ("t", "v", "s", "g", "y")

    LANES = {
        "ld0": ("g", True), "ld0y": ("y", True), "w1pre": ("g", True),
        "w2": ("g", True), "x1st": ("y", False), "misc": ("y", True),
        "ccb": ("g", True), "out": ("y", True),
        "w1s0": ("g", False), "w1s1": ("g", False),
        "x1t0": ("y", False), "x1t1": ("y", False),
        "xrow": ("y", False), "x1ld": ("y", False),
    }

    def __init__(self):
        self.nc = bass.Bass("TRN2", target_bir_lowering=False,
                            detect_race_conditions=not bool(
                                os.environ.get("KNL_NORACE")))
        self.prog = []
        self.esem = {e: self.nc.alloc_semaphore(name="sem_" + e) for e in self.ENG}
        self.cnt = {e: 0 for e in self.ENG}
        self.dsem = {q: self.nc.alloc_semaphore(name="sem_" + q) for q in self.LANES}
        self.dcnt = {q: 0 for q in self.LANES}
        self.ccsem = self.nc.alloc_semaphore(name="sem_cc")
        self.cccnt = 0
        self.hiwater = {e: {} for e in self.ENG}
        self.last_op = {e: None for e in self.ENG}

    def raw(self, e, fn, deps=()):
        for tok in deps:
            self.wait(e, tok)
        self.prog.append([e, fn, []])
        self.last_op[e] = len(self.prog) - 1

    def wait(self, e, tok):
        if tok is None:
            return
        if len(tok) == 3:
            lane, sem, val = tok
            if self.LANES[lane][1]:
                val = self.dcnt[lane]
        else:
            sem, val = tok
        hw = self.hiwater[e]
        k = sem.name
        if hw.get(k, -1) >= val:
            return
        hw[k] = val
        self.prog.append([e, lambda h: h.wait_ge(sem, val), None])

    def sig(self, e):
        self.cnt[e] += 1
        v = self.cnt[e]
        sem = self.esem[e]
        li = self.last_op[e]
        if li is not None and self.prog[li][2] is not None:
            incs = self.prog[li][2]
            for j, (sm, n) in enumerate(incs):
                if sm is sem:
                    incs[j] = (sm, n + 1)
                    break
            else:
                incs.append((sem, 1))
        else:
            self.prog.append([e, lambda h: h.nop().then_inc(sem, 1), None])
        return (sem, v)

    def dma(self, lane, out_ap, in_ap, deps=(), transpose=False):
        eng = self.LANES[lane][0]
        for tok in deps:
            self.wait(eng, tok)
        self.dcnt[lane] += 16
        v = self.dcnt[lane]
        sem = self.dsem[lane]
        if transpose:
            self.prog.append([eng, lambda h: h.dma_start_transpose(
                out=out_ap, in_=in_ap).then_inc(sem, 16), None])
        else:
            self.prog.append([eng, lambda h: h.dma_start(
                out=out_ap, in_=in_ap).then_inc(sem, 16), None])
        return (lane, sem, v)

    def mm(self, psum_ap, lhsT_ap, rhs_ap, start=True, stop=True, deps=()):
        self.raw("t", lambda t: t.matmul(psum_ap, lhsT_ap, rhs_ap,
                                         start=start, stop=stop), deps)

    def tr(self, psum_ap, in_ap, ident_ap, deps=()):
        self.raw("t", lambda t: t.transpose(psum_ap, in_ap, ident_ap), deps)

    def cp(self, e, out, in_, deps=()):
        if e == "s":
            self.raw(e, lambda h: h.copy(out=out, in_=in_), deps)
        else:
            self.raw(e, lambda h: h.tensor_copy(out=out, in_=in_), deps)

    def act(self, out, in_, func, bias=None, scale=None, accum=None, deps=()):
        kw = {}
        if bias is not None:
            kw["bias"] = bias
        if scale is not None:
            kw["scale"] = scale
        if accum is not None:
            kw["accum_out"] = accum
        self.raw("s", lambda h: h.activation(out=out, in_=in_, func=func, **kw),
                 deps)

    def tt(self, e, out, a, b, op, deps=()):
        self.raw(e, lambda h: h.tensor_tensor(out=out, in0=a, in1=b, op=op), deps)

    def ts(self, e, out, in0, s1, op0, s2=None, op1=None, deps=()):
        if op1 is None:
            self.raw(e, lambda h: h.tensor_scalar(out=out, in0=in0, scalar1=s1,
                                                  scalar2=None, op0=op0), deps)
        else:
            self.raw(e, lambda h: h.tensor_scalar(out=out, in0=in0, scalar1=s1,
                                                  scalar2=s2, op0=op0, op1=op1),
                     deps)

    def stt(self, e, out, in0, scalar, in1, op0, op1, deps=()):
        self.raw(e, lambda h: h.scalar_tensor_tensor(
            out=out, in0=in0, scalar=scalar, in1=in1, op0=op0, op1=op1), deps)

    def reduce(self, out, in_, op, deps=()):
        self.raw("v", lambda v: v.tensor_reduce(out=out, in_=in_, axis=AX.X,
                                                op=op), deps)

    def collective(self, kind, op, groups, in_dram, out_dram, deps=()):
        self.cccnt += 1
        v = self.cccnt
        sem = self.ccsem

        def fn(h):
            return h.collective_compute(kind, op, replica_groups=groups,
                                        ins=[in_dram.ap().opt()],
                                        outs=[out_dram.ap().opt()]).then_inc(sem, 1)
        self.raw("g", fn, deps)
        self.prog[self.last_op["g"]][2] = None
        self.last_op["g"] = None
        return (sem, v)

    def build(self):
        nc, prog = self.nc, self.prog

        def run(h, key):
            for ent in prog:
                if ent[0] != key:
                    continue
                inst = ent[1](h)
                if ent[2]:
                    for sem, n in ent[2]:
                        inst = inst.then_inc(sem, n)

        with nc.Block() as block:
            @block.tensor
            def _(t):
                run(t, "t")

            @block.vector
            def _(v):
                run(v, "v")

            @block.scalar
            def _(s):
                run(s, "s")

            @block.gpsimd
            def _(g):
                run(g, "g")

            @block.sync
            def _(y):
                run(y, "y")
        return nc


class Bank:

    def __init__(self, kb, handle):
        self.kb = kb
        self.h = handle
        self.read_tok = None

    def wdeps(self, deps):
        d = list(deps)
        if self.read_tok is not None:
            d.append(self.read_tok)
        return d

    def mm(self, sl, lhsT, rhs, start=True, stop=True, deps=(), rows=None):
        ap = self.h[:, sl] if rows is None else self.h[rows, sl]
        self.kb.mm(ap, lhsT, rhs, start=start, stop=stop,
                   deps=self.wdeps(deps) if start else deps)

    def tr(self, sl, in_ap, ident, deps=(), rows=None):
        ap = self.h[:, sl] if rows is None else self.h[rows, sl]
        self.kb.tr(ap, in_ap, ident, deps=self.wdeps(deps))

    def done(self, tok):
        self.read_tok = tok


def build_full(debug_taps=False):
    PHASE = int(os.environ.get("KNL_PHASE", "9"))
    kb = KB()
    nc = kb.nc
    TAPS = {}

    def finish_early():
        for e in kb.ENG:
            if kb.cnt[e]:
                kb.wait("g", (kb.esem[e], kb.cnt[e]))
                kb.wait("y", (kb.esem[e], kb.cnt[e]))
        for q in kb.LANES:
            if kb.dcnt[q]:
                kb.wait("g", (q, kb.dsem[q], kb.dcnt[q]))
                kb.wait("y", (q, kb.dsem[q], kb.dcnt[q]))
        if os.environ.get("KNL_TAPS"):
            for nm, ap in list(TAPS.items()):
                t = nc.dram_tensor("tap_" + nm, list(ap.shape), ap.dtype,
                                   kind="ExternalOutput")
                kb.dma("out", t.ap(), ap)
        kb.wait("y", ("out", kb.dsem["out"], kb.dcnt["out"]))
        kb.wait("g", ("out", kb.dsem["out"], kb.dcnt["out"]))
        return kb.build()

    din = lambda nm, shp: nc.dram_tensor(nm, shp, F32, kind="ExternalInput")
    xT_h = din("xT", [D, TOK])
    xrows_h = din("xrows", [TOK, D])
    Wq_h = din("Wq", [D, C]); Wk_h = din("Wk", [D, C])
    Wv_h = din("Wv", [D, C]); Wg_h = din("Wg", [D, C])
    Wab_h = din("Wab", [D, 2]); Wo_h = din("Wo", [C, D])
    bqkvg_h = din("bqkvg", [C, 4]); bab_h = din("bab", [2, 1])
    bo_h = din("bo_row", [1, D])
    wg16_h = din("wg16", [D, E]); bg16_h = din("bg16", [E, 1])
    w1_h = din("w1", [E, D, FP]); b1c_h = din("b1c", [128, 32])
    w2_h = din("w2", [E, FP, D]); b2_h = din("b2r", [E, D])
    gam_h = din("gamma_row", [1, D]); bet_h = din("beta_row", [1, D])
    hm_h = din("hmask", [128, 1]); ep_h = din("eps_role", [128, 1])
    pm8_h = din("pm8", [8, 1]); bm8_h = din("bm8", [8, 1])
    out_h = nc.dram_tensor("out", [TOK, D], F32, kind="ExternalOutput")

    cb_d = nc.dram_tensor("cbounce", [2, TOK], F32)
    x1_d = nc.dram_tensor("x1rows", [TOK, D], BF16)
    zin_d = nc.dram_tensor("z_ar_in", [C, C], F32)
    zout_d = nc.dram_tensor("z_ar_out", [C, C], F32)
    cin_d = nc.dram_tensor("cnt_ag_in", [E, 1], F32)
    cout_d = nc.dram_tensor("cnt_ag_out", [8, E], F32)
    crd = nc.dram_tensor("cntrow_d", [1, 32], F32)
    w1bf_d = nc.dram_tensor("w1bf", [E, 128, DC * FP], BF16)

    sb = lambda nm, shp, dt=F32: nc.alloc_sbuf_tensor(nm, shp, dt)

    W2R = sb("W2R", [128, 32768], BF16)
    HR = sb("HR", [128, 16384], BF16)
    OR_ = sb("OR", [128, 8192], BF16)
    W2Rf = W2R.bitcast(F32)
    HRf = HR.bitcast(F32)
    ORf = OR_.bitcast(F32)

    w2b = W2R
    xTb = W2R[:, 0:16384]
    KtA = W2R[:, 16384:18432]
    WTA = W2R[:, 18432:20480]
    UeA = W2R[:, 20480:22528]
    A2A = W2R[:, 22528:24576]
    abT = W2Rf[0:2, 12288:14336]
    c_row = W2Rf[0:1, 14336:16384]

    hbuf = HR
    qT = HR[:, 0:2048]
    ktT = HR[:, 2048:4096]
    vT = HR[:, 4096:6144]
    gT = HR[:, 6144:8192]
    ogT = HR[:, 8192:10240]
    tmpP = HR[:, 10240:14336]
    tmpPf = HRf[:, 5120:7168]
    tmp16 = HRf[0:16, 5120:7168]

    _bofs = [0]
    def cb(n=128):
        o = _bofs[0]; _bofs[0] += n
        return tmpP[:, o:o + n]
    gamt = cb(); gamtT = cb(); Pt = cb(); PTt = cb(); P2t = cb(); PT2t = cb()
    Rt = cb(); RTt = cb(); GRT = cb(); Bk = cb(); Bv = cb(); UTt = cb()
    WZs = cb(); IgL = cb(); MqT = cb(); T4n = cb(); gbc = cb(); scrb = cb()
    _fofs = [5120 + 2304 // 2]
    def cf(n=128):
        o = _fofs[0]; _fofs[0] += n
        return HRf[:, o:o + n]
    dm = cf(); scr = cf(); e1t = cf(); OA = cf(); Ot1 = cf()

    ostg = ORf[:, 0:4096]
    scoresT = ORf[0:16, 0:2048]
    zT16 = ORf[0:16, 2048:4096]
    nrm = ORf[0:1, 0:2048]
    nrmb = OR_[0:1, 4096:6144]
    crb = OR_[0:1, 6144:8192]
    growR = ORf[0:1, 0:2048]
    growRb = OR_[0:1, 4096:6144]
    iotE = ORf[0:16, 0:2048]

    wqb = sb("wqb", [128, DC * C], BF16)
    wkb = sb("wkb", [128, DC * C], BF16)
    wvb = sb("wvb", [128, DC * C], BF16)
    wgb = sb("wgb", [128, DC * C], BF16)
    wabb = sb("wabb", [128, DC * 2], BF16)
    wob = sb("wob", [128, D], BF16)
    wg16b = sb("wg16b", [128, DC * E], BF16)
    bqkvg = sb("bqkvgS", [C, 4], F32)
    bab = sb("babS", [2, 1], F32)
    borow = sb("borowS", [1, D], BF16)
    bg16c = sb("bg16S", [E, 1], F32)
    b1c = sb("b1cS", [128, 32], F32)
    b2r = sb("b2rS", [E, D], BF16)
    grow_ = sb("gamrowB", [1, D], BF16)
    brow_ = sb("betrowB", [1, D], BF16)
    gamb = sb("gammaB", [128, D], BF16)
    betb = sb("betaB", [128, D], BF16)
    selT = sb("selT", [E, TOK], BF16)
    maskT = sb("maskT", [E, TOK], BF16)
    hmcol = sb("hmcol", [128, 1], F32)
    epcol = sb("epcol", [128, 1], F32)
    pm8c = sb("pm8c", [8, 1], F32)
    bm8c = sb("bm8c", [8, 1], F32)
    Ibf = sb("Ibf", [128, 128], BF16)
    NEGSL = sb("NEGSL", [128, 128], BF16)
    TRIU = sb("TRIU", [128, 128], BF16)
    NTRIU = sb("NTRIU", [128, 128], BF16)
    iot = sb("iot", [128, 128], F32)
    If32 = sb("If32", [128, 128], F32)
    ones1 = sb("ones1", [1, 512], BF16)
    ones1f = sb("ones1f", [1, 128], F32)
    onesc = sb("onesc", [128, 1], BF16)
    xrow_t = sb("xrow_t", [128, D], F32)
    zrow = sb("zrow", [128, 2 * D], F32)
    x1tok = sb("x1tok", [128, D], BF16)
    x1ld = sb("x1ld", [128, D], BF16)
    outf = sb("outf", [128, D], F32)
    t2b = sb("t2b", [128, 2 * D], BF16)
    sumc = sb("sumc", [128, 2], F32)
    sqc = sb("sqc", [128, 2], F32)
    mucol = sb("mucol", [128, 2], F32)
    varcol = sb("varcol", [128, 2], F32)
    rstdc = sb("rstdc", [128, 2], F32)
    epsc = sb("epsc", [128, 1], F32)
    nc.const_aps.register(F32, LNEPS, epsc[:, 0:1])
    k2 = sb("k2scr", [128, 512], BF16)
    cbx = sb("cbx", [128, 2 * NT], F32)
    prow = sb("prow", [1, 2 * NT], F32)
    pebf = sb("pebf", [1, 2 * NT], BF16)
    gldiff = sb("gldiff", [1, NT], F32)
    gamrow = sb("gamrow", [1, NT], F32)
    gamcol = sb("gamcol", [128, NT], F32)
    ecol = sb("ecol", [128, NT], F32)
    gLbc = sb("gLbc", [128, NT], F32)
    IdE = sb("IdE", [E, E * 128], BF16)
    Zc = sb("Zc", [128, 128], BF16)
    ZTc = sb("ZTc", [128, 128], BF16)
    Zf32 = sb("Zf32", [128, 128], F32)
    sctok = sb("sctok", [128, 2 * E], F32)
    m2col = sb("m2col", [128, 2 * 8], F32)
    msk_tok = sb("msk_tok", [128, 2 * 128], BF16)
    sc_stage_f = sb("sc_stage_f", [128, 128], F32)
    bgrow = sb("bgrow", [1, E], BF16)
    nposl = sb("nposl", [E, 1], F32)
    cnt8 = sb("cnt8", [8, E], F32)
    nzl = sb("nzl", [8, E], F32)
    cntrow = sb("cntrow", [1, 32], F32)
    offsc = sb("offsc", [E, 1], F32)
    padc = sb("padc", [E, 1], F32)
    x1T = sb("x1T", [128, 2 * DC * 512], BF16)
    w1cur = sb("w1cur", [128, 3 * 2048], BF16)
    w1stg = OR_[:, 6144:8192]

    B = [Bank(kb, nc.alloc_psum_tensor("ps%d" % i, [128, 512], F32))
         for i in range(7)]
    BT = Bank(kb, nc.alloc_psum_tensor("psT", [128, 1024], BF16))

    S512 = slice(0, 512)

    xld = [kb.dma("ld0", xTb[:, dc * TOK:(dc + 1) * TOK],
                  xT_h.ap()[dc * 128:(dc + 1) * 128, :]) for dc in range(DC)]
    wq_t = kb.dma("ld0", wqb[:], bass.AP(Wq_h, 0, [[C, 128], [128 * C, DC], [1, C]]))
    wk_t = kb.dma("ld0", wkb[:], bass.AP(Wk_h, 0, [[C, 128], [128 * C, DC], [1, C]]))
    wv_t = kb.dma("ld0", wvb[:], bass.AP(Wv_h, 0, [[C, 128], [128 * C, DC], [1, C]]))
    wgt_t = kb.dma("ld0", wgb[:], bass.AP(Wg_h, 0, [[C, 128], [128 * C, DC], [1, C]]))
    wab_t = kb.dma("ld0", wabb[:], bass.AP(Wab_h, 0, [[2, 128], [128 * 2, DC], [1, 2]]))
    wo_t = kb.dma("ld0", wob[:], Wo_h.ap())
    bo_t = kb.dma("ld0", borow[:], bo_h.ap())
    bq_t = kb.dma("ld0y", bqkvg[:], bqkvg_h.ap())
    ba_t = kb.dma("ld0y", bab[:], bab_h.ap())
    ep_t = kb.dma("ld0y", epcol[:], ep_h.ap())
    hm_t = kb.dma("ld0y", hmcol[:], hm_h.ap())
    pm_t = kb.dma("ld0y", pm8c[:], pm8_h.ap())
    bmm_t = kb.dma("ld0y", bm8c[:], bm8_h.ap())
    wgg_t = kb.dma("ld0", wg16b[:], bass.AP(wg16_h, 0, [[E, 128], [128 * E, DC], [1, E]]))
    bg_t = kb.dma("ld0y", bg16c[:], bg16_h.ap())
    bgr_t = kb.dma("ld0", bgrow[0:1, :], bass.AP(bg16_h, 0, [[E, 1], [1, E]]))
    b1_t = kb.dma("ld0y", b1c[:], b1c_h.ap())
    b2_t = kb.dma("ld0", b2r[:], b2_h.ap())
    gr_t = kb.dma("ld0", grow_[:], gam_h.ap())
    br_t = kb.dma("ld0", brow_[:], bet_h.ap())
    w1pre = []
    prev = None
    for e in range(E):
        d1 = kb.dma("w1pre", w1stg[:],
                    bass.AP(w1_h, e * D * FP, [[FP, 128], [128 * FP, DC], [1, FP]]),
                    deps=[prev] if prev else ())
        d2 = kb.dma("w1pre", w1bf_d.ap()[e], w1stg[:], deps=[d1])
        w1pre.append(d2)
        prev = d2

    kb.raw("g", lambda g: g.iota(iot[:], pattern=[[-1, 128]], base=0,
                                 channel_multiplier=1,
                                 allow_small_or_imprecise_dtypes=True))
    kb.raw("g", lambda g: g.memset(ones1[:], 1.0))
    kb.raw("g", lambda g: g.memset(ones1f[:], 1.0))
    kb.raw("g", lambda g: g.memset(onesc[:], 1.0))
    kb.raw("g", lambda g: g.memset(outf[0:1, :], 0.0))
    kb.raw("g", lambda g: g.memset(sc_stage_f[:], 0.0))
    kb.raw("g", lambda g: g.memset(msk_tok[:], 0.0))
    kb.raw("g", lambda g: g.memset(Zc[:], 0.0))
    kb.raw("g", lambda g: g.memset(epsc[:], LNEPS))
    kb.raw("g", lambda g: g.iota(iotE[0:E, :], pattern=[[1, E * 128]], base=0,
                                 channel_multiplier=-128,
                                 allow_small_or_imprecise_dtypes=True))
    g_init = kb.sig("g")
    kb.ts("v", Ibf[:], iot[:], 0.0, ALU.is_equal, deps=[g_init])
    kb.ts("v", If32[:], iot[:], 0.0, ALU.is_equal)
    kb.ts("v", NEGSL[:], iot[:], 0.0, ALU.is_gt, s2=-1.0, op1=ALU.mult)
    kb.ts("v", TRIU[:], iot[:], 0.0, ALU.is_le)
    kb.ts("v", NTRIU[:], iot[:], 0.0, ALU.is_le, s2=-1.0, op1=ALU.mult)
    kb.ts("v", zT16[0:E, :], iotE[0:E, :], -0.5, ALU.is_gt, deps=[g_init])
    kb.ts("v", iotE[0:E, :], iotE[0:E, :], 127.5, ALU.is_le)
    negE = kb.sig("v")
    kb.tt("v", IdE[0:E, :], zT16[0:E, :], iotE[0:E, :], ALU.mult, deps=[negE])
    msk_done = kb.sig("v")

    proj_list = [(wqb, 0, qT, wq_t, "copy"), (wkb, 1, ktT, wk_t, "copy"),
                 (wvb, 2, vT, wv_t, "copy"), (wgb, 3, gT, wgt_t, "sigmoid")]
    ev = {}
    for wsb, bi, dst, wtok, mode in proj_list:
        for tc in range(4):
            dep0 = xld + [wtok]
            for dc in range(DC):
                B[tc].mm(S512, wsb[:, dc * C:(dc + 1) * C],
                         xTb[:, dc * TOK + tc * 512: dc * TOK + (tc + 1) * 512],
                         start=(dc == 0), stop=(dc == DC - 1),
                         deps=dep0 if dc == 0 else ())
        mmt = kb.sig("t")
        for tc in range(4):
            sl = slice(tc * 512, (tc + 1) * 512)
            fn = AF.Sigmoid if mode == "sigmoid" else AF.Identity
            kb.act(dst[:, sl], B[tc].h[:, S512], fn, bias=bqkvg[:, bi:bi + 1],
                   deps=[mmt, bq_t])
        etok = kb.sig("s")
        for tc in range(4):
            B[tc].done(etok)
        ev[bi] = etok
    for tc in range(4):
        for dc in range(DC):
            B[4].mm(S512, wabb[:, dc * 2:(dc + 1) * 2],
                    xTb[:, dc * TOK + tc * 512: dc * TOK + (tc + 1) * 512],
                    start=(dc == 0), stop=(dc == DC - 1),
                    deps=(xld + [wab_t]) if dc == 0 else (), rows=slice(0, 2))
        abmm = kb.sig("t")
        kb.act(abT[:, tc * 512:(tc + 1) * 512], B[4].h[0:2, S512], AF.Sigmoid,
               bias=bab[:, 0:1], deps=[abmm, ba_t])
        B[4].done(kb.sig("s"))
    ab_done = kb.sig("s")
    w2_toks = {}
    for e in range(8):
        w2_toks[e] = kb.dma("w2", w2b[:, e * 2 * D:(e + 1) * 2 * D],
                            bass.AP(w2_h, e * FP * D, [[D, 128], [128 * D, 2], [1, D]]),
                            deps=[ab_done])

    k2_rd = None
    for tc in range(4):
        sl = slice(tc * 512, (tc + 1) * 512)
        kb.act(k2[:, :], ktT[:, sl], AF.Square,
               deps=[ev[1], g_init] + ([k2_rd] if k2_rd else []))
        sqt = kb.sig("s")
        B[5].mm(slice(0, 512), onesc[:, 0:1], k2[:, :], deps=[sqt, g_init],
                rows=slice(0, 1))
        mt = kb.sig("t")
        k2_rd = mt
        kb.cp("v", nrm[0:1, sl], B[5].h[0:1, S512], deps=[mt, msk_done])
        B[5].done(kb.sig("v"))
    nrm_done = kb.sig("v")
    kb.act(nrm[0:1, :], nrm[0:1, :], AF.Sqrt, deps=[nrm_done])
    st = kb.sig("s")
    kb.ts("v", nrm[0:1, :], nrm[0:1, :], 1e-12, ALU.max, deps=[st])
    kb.raw("v", lambda v: v.reciprocal(out=nrm[0:1, :], in_=nrm[0:1, :]))
    kb.cp("v", nrmb[0:1, :], nrm[0:1, :])
    rn_done = kb.sig("v")
    for tc in range(4):
        sl = slice(tc * 512, (tc + 1) * 512)
        B[5].mm(S512, ones1[0:1, 0:128], nrmb[0:1, sl], deps=[rn_done])
        bct = kb.sig("t")
        kb.tt("v", ktT[:, sl], ktT[:, sl], B[5].h[:, S512], ALU.mult, deps=[bct])
        B[5].done(kb.sig("v"))
    ktn_done = kb.sig("v")

    kb.act(c_row[0:1, :], abT[0:1, :], AF.Ln, deps=[ab_done])
    lnt = kb.sig("s")
    kb.raw("v", lambda v: v.tensor_tensor_scan(
        out=c_row[0:1, 0:1024], data0=c_row[0:1, 0:1024], data1=outf[0:1, 0:1024],
        initial=0.0, op0=ALU.add, op1=ALU.add), deps=[lnt, g_init])
    sc1 = kb.sig("v")
    kb.raw("v", lambda v: v.tensor_tensor_scan(
        out=c_row[0:1, 1024:2048], data0=c_row[0:1, 1024:2048], data1=outf[0:1, 0:1024],
        initial=c_row[0:1, 1023:1024], op0=ALU.add, op1=ALU.add), deps=[sc1])
    c_done = kb.sig("v")
    bt1 = kb.dma("misc", cb_d.ap()[0:1, :], c_row[0:1, :], deps=[c_done])
    bt2 = kb.dma("misc", cb_d.ap()[1:2, :], abT[1:2, :], deps=[ab_done])
    btr = kb.dma("misc", sc_stage_f[0:32, :],
                 bass.AP(cb_d, 0, [[128, 32], [1, 128]]), deps=[bt1, bt2, g_init])
    kb.wait("t", btr)
    B[6].tr(slice(0, 128), sc_stage_f[:, :], If32[:], deps=[msk_done])
    cbt = kb.sig("t")
    kb.cp("s", cbx[:, 0:32], B[6].h[:, 0:32], deps=[cbt])
    B[6].done(kb.sig("s"))
    bt3 = kb.sig("s")
    kb.cp("v", prow[0:1, 0:2 * NT:2], c_row[0:1, 127:TOK:128], deps=[c_done])
    kb.raw("v", lambda v: v.memset(prow[0:1, 1:2], 0.0))
    kb.cp("v", prow[0:1, 3:2 * NT:2], c_row[0:1, 127:TOK - 128:128])
    prow_done = kb.sig("v")
    kb.tt("v", gldiff[0:1, :], prow[0:1, 0:2 * NT:2], prow[0:1, 1:2 * NT:2],
          ALU.subtract, deps=[prow_done])
    gld = kb.sig("v")
    kb.act(gamrow[0:1, :], gldiff[0:1, :], AF.Exp, deps=[gld])
    B[6].mm(slice(0, 2 * NT), ones1f[0:1, :], prow[0:1, :], deps=[prow_done])
    bcp = kb.sig("t")
    kb.tt("v", gamcol[:], cbx[:, 0:NT], B[6].h[:, 1:2 * NT:2], ALU.subtract,
          deps=[bcp, bt3])
    g1 = kb.sig("v")
    kb.act(gamcol[:], gamcol[:], AF.Exp, deps=[g1])
    kb.tt("v", ecol[:], B[6].h[:, 0:2 * NT:2], cbx[:, 0:NT], ALU.subtract,
          deps=[bcp])
    g2 = kb.sig("v")
    kb.act(ecol[:], ecol[:], AF.Exp, deps=[g2])
    B[6].done(kb.sig("v"))
    kb.cp("v", pebf[0:1, 0:NT], gamrow[0:1, :], deps=[kb.sig("s")])
    grb = kb.sig("v")
    B[6].mm(slice(64, 64 + NT), ones1[0:1, 0:128], pebf[0:1, 0:NT], deps=[grb])
    glm = kb.sig("t")
    kb.cp("s", gLbc[:], B[6].h[:, 64:64 + NT], deps=[glm])
    B[6].done(kb.sig("s"))
    decay_s = kb.sig("s")
    pv_ap = bass.AP(prow, 1, [[2 * NT, 1], [2, NT], [0, 128]])
    kb.cp("v", growR, pv_ap, deps=[prow_done, ktn_done])
    pvr = kb.sig("v")
    kb.tt("v", growR, c_row, growR, ALU.subtract, deps=[pvr, c_done])
    grs = kb.sig("v")
    kb.act(growR, growR, AF.Exp, deps=[grs])
    gre = kb.sig("s")
    kb.cp("v", growRb, growR, deps=[gre, ktn_done])
    growNT_done = kb.sig("v")

    for e in range(12, 14):
        w2_toks[e] = kb.dma("w2", w2b[:, e * 2 * D:(e + 1) * 2 * D],
                            bass.AP(w2_h, e * FP * D, [[D, 128], [128 * D, 2], [1, D]]),
                            deps=[growNT_done, bt3, c_done])

    prep = [ktn_done, decay_s, growNT_done, msk_done, ev[0], ev[2], ev[3]]
    TAPS.update(qT=qT[:], ktT=ktT[:], vT=vT[:], gT=gT[:], cbx=cbx[:],
                prow=prow[0:1, :], gamrow=gamrow[0:1, :], gamcol=gamcol[:],
                ecol=ecol[:], gLbc=gLbc[:], growR=growR, c_row=c_row,
                nrm=nrm)
    if PHASE <= 1:
        return finish_early()

    chA = [g_init]
    NCH = int(os.environ.get("KNL_NCH", str(NT)))
    STEP = int(os.environ.get("KNL_STEP", "99"))
    for ch in range(NCH):
        csl = slice(ch * 128, (ch + 1) * 128)
        b_col = cbx[:, NT + ch:NT + ch + 1]
        ccol_b = bass.AP(cbx, ch, [[2 * NT, 128], [0, 128]])
        B[0].mm(slice(0, 128), ktT[:, csl], ktT[:, csl],
                deps=prep if ch == 0 else [ktn_done])
        gmm = kb.sig("t")
        B[1].mm(slice(0, 128), ones1f[0:1, :], c_row[0:1, csl], deps=[c_done])
        cbm = kb.sig("t")
        kb.stt("v", dm[:], B[1].h[:, 0:128], -1.0, ccol_b, ALU.mult, ALU.add,
               deps=[cbm, bt3])
        B[1].done(kb.sig("v"))
        dmt = kb.sig("v")
        kb.ts("v", scr[:], dm[:], 0.0, ALU.min, deps=[dmt])
        s1 = kb.sig("v")
        kb.act(gamt[:], scr[:], AF.Exp, deps=[s1])
        gam_s = kb.sig("s")
        if STEP < 1:
            continue
        kb.ts("v", scr[:], dm[:], -1.0, ALU.mult, s2=0.0, op1=ALU.min, deps=[gam_s])
        s2_ = kb.sig("v")
        kb.act(gamtT[:], scr[:], AF.Exp, deps=[s2_])
        gamT_s = kb.sig("s")
        if STEP < 2:
            continue
        kb.stt("v", Pt[:], B[0].h[:, 0:128], b_col, NEGSL[:], ALU.mult, ALU.mult,
               deps=[gmm, bt3])
        B[0].done(kb.sig("v"))
        Ptok = kb.sig("v")
        BT.tr(slice(0, 128), Pt[:], Ibf[:], deps=[Ptok])
        ptm = kb.sig("t")
        kb.cp("s", PTt[:], BT.h[:, 0:128], deps=[ptm])
        BT.done(kb.sig("s"))
        PTtok = kb.sig("s")
        if STEP < 3:
            continue
        kb.tt("v", Rt[:], Pt[:], Ibf[:], ALU.add, deps=[Ptok])
        Rtok = kb.sig("v")
        curP, curPT, nxtP, nxtPT = Pt, PTt, P2t, PT2t
        curPtok, curPTtok = Ptok, PTtok
        for lev in range(NLEV):
            B[0].mm(slice(0, 128), curPT[:], curP[:], deps=[curPTtok, curPtok])
            sq1 = kb.sig("t")
            kb.cp("s", nxtP[:], B[0].h[:, 0:128], deps=[sq1])
            B[0].done(kb.sig("s"))
            nPtok = kb.sig("s")
            if lev < NLEV - 1:
                B[1].mm(slice(0, 128), curP[:], curPT[:],
                        deps=[curPtok, curPTtok])
                sq2 = kb.sig("t")
                kb.cp("s", nxtPT[:], B[1].h[:, 0:128], deps=[sq2])
                B[1].done(kb.sig("s"))
                nPTtok = kb.sig("s")
            else:
                BT.tr(slice(0, 128), nxtP[:], Ibf[:], deps=[nPtok])
                sqt = kb.sig("t")
                kb.cp("s", nxtPT[:], BT.h[:, 0:128], deps=[sqt])
                BT.done(kb.sig("s"))
                nPTtok = kb.sig("s")
            curP, curPT, nxtP, nxtPT = nxtP, nxtPT, curP, curPT
            curPtok, curPTtok = nPtok, nPTtok
            B[6].mm(slice(0, 128), curPT[:], Rt[:], deps=[curPTtok, Rtok])
            rum = kb.sig("t")
            kb.tt("v", Rt[:], B[6].h[:, 0:128], Rt[:], ALU.add, deps=[rum])
            B[6].done(kb.sig("v"))
            Rtok = kb.sig("v")
        if STEP < 4:
            continue
        BT.tr(slice(128, 256), Rt[:], Ibf[:], deps=[Rtok])
        rtm = kb.sig("t")
        kb.cp("s", RTt[:], BT.h[:, 128:256], deps=[rtm])
        BT.done(kb.sig("s"))
        RTtok = kb.sig("s")
        kb.tt("v", GRT[:], RTt[:], gamtT[:], ALU.mult, deps=[RTtok, gamT_s])
        GRTtok = kb.sig("v")
        if STEP < 5:
            continue
        BT.tr(slice(256, 384), ktT[:, csl], Ibf[:], deps=[ktn_done])
        ktm = kb.sig("t")
        kb.cp("s", KtA[:, csl], BT.h[:, 256:384], deps=[ktm])
        Kttok = kb.sig("s")
        kb.ts("v", Bk[:], BT.h[:, 256:384], b_col, ALU.mult,
              deps=[ktm, bt3, Kttok])
        Bktok = kb.sig("v")
        BT.tr(slice(384, 512), vT[:, csl], Ibf[:], deps=[ev[2], Bktok, Kttok])
        vtm = kb.sig("t")
        kb.ts("v", Bv[:], BT.h[:, 384:512], b_col, ALU.mult, deps=[vtm])
        Bvtok = kb.sig("v")
        BT.done(Bvtok)
        if STEP < 6:
            continue
        B[2].mm(slice(0, 128), Bk[:], RTt[:], deps=[Bktok, RTtok])
        wtm = kb.sig("t")
        kb.cp("s", WTA[:, csl], B[2].h[:, 0:128], deps=[wtm])
        B[2].done(kb.sig("s"))
        WTtok = kb.sig("s")
        B[3].mm(slice(0, 128), GRT[:], Bv[:], deps=[GRTtok, Bvtok])
        um = kb.sig("t")
        kb.ts("v", UeA[:, csl], B[3].h[:, 0:128], ecol[:, ch:ch + 1], ALU.mult,
              deps=[um, decay_s])
        B[3].done(kb.sig("v"))
        Uetok = kb.sig("v")
        B[4].mm(slice(0, 128), Bv[:], GRT[:], deps=[GRTtok, Bvtok])
        utm = kb.sig("t")
        kb.cp("s", UTt[:], B[4].h[:, 0:128], deps=[utm])
        B[4].done(kb.sig("s"))
        UTtok = kb.sig("s")
        if STEP < 7:
            continue
        B[5].mm(slice(0, 128), UTt[:], qT[:, csl], deps=[UTtok, ev[0]])
        t5m = kb.sig("t")
        kb.tt("v", scr[:], B[5].h[:, 0:128], gamtT[:], ALU.mult, deps=[t5m])
        B[5].done(kb.sig("v"))
        a21 = kb.sig("v")
        kb.tt("v", A2A[:, csl], scr[:], TRIU[:], ALU.mult, deps=[a21])
        A2tok = kb.sig("v")
        if STEP < 8:
            continue
        kb.ts("v", IgL[:], Ibf[:], gLbc[:, ch:ch + 1], ALU.mult, deps=[decay_s])
        igl = kb.sig("v")
        B[2].mm(slice(128, 256), WTA[:, csl], Zc[:], deps=[WTtok] + chA)
        wzm = kb.sig("t")
        kb.ts("v", WZs[:], B[2].h[:, 128:256], gLbc[:, ch:ch + 1], ALU.mult,
              s2=-1.0, op1=ALU.mult, deps=[wzm])
        B[2].done(kb.sig("v"))
        wzs = kb.sig("v")
        B[4].mm(slice(128, 256), IgL[:], Zc[:], start=True, stop=False,
                deps=[igl] + chA)
        B[4].mm(slice(128, 256), KtA[:, csl], WZs[:], start=False, stop=False,
                deps=[wzs, Kttok])
        B[4].mm(slice(128, 256), KtA[:, csl], UeA[:, csl], start=False, stop=True,
                deps=[Uetok])
        znm = kb.sig("t")
        kb.cp("s", Zc[:], B[4].h[:, 128:256], deps=[znm])
        B[4].done(kb.sig("s"))
        chA = [kb.sig("s")]

    if NCH < NT:
        return finish_early()
    for e in range(14, 16):
        w2_toks[e] = kb.dma("w2", w2b[:, e * 2 * D:(e + 1) * 2 * D],
                            bass.AP(w2_h, e * FP * D, [[D, 128], [128 * D, 2], [1, D]]),
                            deps=chA)

    kb.ts("v", Zf32[:], Zc[:], hmcol[:, 0:1], ALU.mult, deps=chA + [hm_t])
    zf = kb.sig("v")
    ar1 = kb.dma("ccb", zin_d.ap(), Zf32[:], deps=[zf])
    cc1 = kb.collective("AllReduce", ALU.add, [[0, 1], [2, 3], [4, 5], [6, 7]],
                        zin_d, zout_d, deps=[ar1])
    ar2 = kb.dma("ccb", Zf32[:], zout_d.ap(), deps=[cc1])
    kb.ts("v", Zc[:], Zf32[:], epcol[:, 0:1], ALU.mult, deps=[ar2, ep_t])
    z0 = kb.sig("v")
    BT.tr(slice(0, 128), Zc[:], Ibf[:], deps=[z0])
    ztm = kb.sig("t")
    kb.cp("s", ZTc[:], BT.h[:, 0:128], deps=[ztm])
    BT.done(kb.sig("s"))
    chB = [kb.sig("s"), z0]
    TAPS.update(KtA=KtA[:], WTA=WTA[:], UeA=UeA[:], A2A=A2A[:], Zc=Zc[:],
                ZTc=ZTc[:], Zf32=Zf32[:])
    if PHASE <= 2:
        return finish_early()

    for ch in range(NT):
        csl = slice(ch * 128, (ch + 1) * 128)
        B[1].mm(slice(0, 128), KtA[:, csl], A2A[:, csl], deps=chB)
        iam = kb.sig("t")
        kb.cp("s", OA[:], B[1].h[:, 0:128], deps=[iam])
        B[1].done(kb.sig("s"))
        oat = kb.sig("s")
        B[2].mm(slice(0, 128), ZTc[:], qT[:, csl], start=True, stop=True,
                deps=chB)
        mqm = kb.sig("t")
        kb.cp("v", MqT[:], B[2].h[:, 0:128], deps=[mqm])
        mqt = kb.sig("v")
        B[3].mm(slice(0, 128), WTA[:, csl], MqT[:], deps=[mqt])
        t4m = kb.sig("t")
        kb.tt("v", T4n[:], B[3].h[:, 0:128], NTRIU[:], ALU.mult, deps=[t4m])
        B[3].done(kb.sig("v"))
        t4t = kb.sig("v")
        B[6].mm(slice(0, 128), KtA[:, csl], T4n[:], deps=[t4t])
        bbm = kb.sig("t")
        B[6].mm(slice(384, 512), ones1[0:1, 0:128],
                growRb[0:1, ch * 128:(ch + 1) * 128], deps=[growNT_done])
        gbm = kb.sig("t")
        kb.cp("s", gbc[:], B[6].h[:, 384:512], deps=[gbm])
        gbct = kb.sig("s")
        kb.tt("v", Ot1[:], B[6].h[:, 0:128], MqT[:], ALU.add, deps=[bbm, gbct])
        B[6].done(kb.sig("v"))
        B[2].done(mqt)
        kb.tt("v", Ot1[:], Ot1[:], gbc[:], ALU.mult)
        o1t = kb.sig("v")
        kb.tt("v", Ot1[:], Ot1[:], OA[:], ALU.add, deps=[oat, o1t])
        o2t = kb.sig("v")
        kb.tt("v", ogT[:, csl], Ot1[:], gT[:, csl], ALU.mult, deps=[o2t, ev[3]])
        kb.sig("v")
        kb.ts("v", IgL[:], Ibf[:], gLbc[:, ch:ch + 1], ALU.mult)
        igl = kb.sig("v")
        B[0].mm(slice(128, 256), WTA[:, csl], Zc[:], deps=chB)
        wzm = kb.sig("t")
        kb.ts("v", WZs[:], B[0].h[:, 128:256], gLbc[:, ch:ch + 1], ALU.mult,
              s2=-1.0, op1=ALU.mult, deps=[wzm])
        B[0].done(kb.sig("v"))
        wzs = kb.sig("v")
        B[4].mm(slice(128, 256), IgL[:], Zc[:], start=True, stop=False,
                deps=[igl] + chB)
        B[4].mm(slice(128, 256), KtA[:, csl], WZs[:], start=False, stop=False,
                deps=[wzs])
        B[4].mm(slice(128, 256), KtA[:, csl], UeA[:, csl], start=False, stop=True)
        znm = kb.sig("t")
        B[5].mm(slice(128, 256), IgL[:], ZTc[:], start=True, stop=False,
                deps=[igl] + chB)
        B[5].mm(slice(128, 256), WZs[:], KtA[:, csl], start=False, stop=False,
                deps=[wzs])
        B[5].mm(slice(128, 256), UeA[:, csl], KtA[:, csl], start=False, stop=True)
        ztnm = kb.sig("t")
        kb.cp("s", Zc[:], B[4].h[:, 128:256], deps=[znm])
        B[4].done(kb.sig("s"))
        kb.cp("s", ZTc[:], B[5].h[:, 128:256], deps=[ztnm])
        B[5].done(kb.sig("s"))
        chB = [kb.sig("s")]
    og_all = kb.sig("v")
    for e in range(8, 12):
        w2_toks[e] = kb.dma("w2", w2b[:, e * 2 * D:(e + 1) * 2 * D],
                            bass.AP(w2_h, e * FP * D, [[D, 128], [128 * D, 2], [1, D]]),
                            deps=chB + [og_all])

    if PHASE <= 3:
        return finish_early()
    x1_st = []
    xrow_rd = [None, None]
    zrow_rd = [None, None]
    x1tok_rd = [None, None]
    col_rd = [None, None]
    for ti in range(NT):
        tp = ti % 2
        pc = slice(tp, tp + 1)
        tsl = slice(ti * 128, (ti + 1) * 128)
        bk0, bk1 = (B[0], B[1]) if tp == 0 else (B[2], B[3])
        xr = kb.dma("xrow", xrow_t[:, :], xrows_h.ap()[tsl, :],
                    deps=[xrow_rd[0]] if xrow_rd[0] else ())
        for dh, bk in ((0, bk0), (1, bk1)):
            bk.mm(S512, ogT[:, tsl], wob[:, dh * 512:(dh + 1) * 512],
                  start=True, stop=False, deps=[og_all, wo_t] + chB)
            bk.mm(S512, ones1[0:1, 0:128], borow[0:1, dh * 512:(dh + 1) * 512],
                  start=False, stop=True, deps=[bo_t])
        ymm = kb.sig("t")
        zdeps = [ymm, xr] + ([zrow_rd[tp]] if zrow_rd[tp] else [])
        kb.tt("v", zrow[:, tp * D:tp * D + 512], bk0.h[:, S512],
              xrow_t[:, 0:512], ALU.add, deps=zdeps)
        kb.tt("v", zrow[:, tp * D + 512:(tp + 1) * D], bk1.h[:, S512],
              xrow_t[:, 512:1024], ALU.add, deps=[ymm, xr])
        ztok = kb.sig("v")
        zsl = zrow[:, tp * D:(tp + 1) * D]
        xrow_rd[0] = ztok
        bk0.done(ztok)
        bk1.done(ztok)
        cdeps = [ztok] + ([col_rd[tp]] if col_rd[tp] else [])
        kb.reduce(sumc[:, pc], zsl, ALU.add, deps=cdeps)
        s1 = kb.sig("v")
        kb.act(outf[:, :], zsl, AF.Square, accum=sqc[:, pc], deps=cdeps)
        s2_ = kb.sig("s")
        kb.ts("v", mucol[:, pc], sumc[:, pc], 1.0 / D, ALU.mult, deps=[s1])
        kb.ts("v", varcol[:, pc], sqc[:, pc], 1.0 / D, ALU.mult, deps=[s2_])
        vready = kb.sig("v")
        kb.stt("v", varcol[:, pc], mucol[:, pc], mucol[:, pc], varcol[:, pc],
               ALU.mult, ALU.subtract, deps=[vready])
        nv = kb.sig("v")
        kb.act(rstdc[:, pc], varcol[:, pc], AF.Sqrt, scale=-1.0, bias=LNEPS,
               deps=[nv])
        sr = kb.sig("s")
        kb.raw("v", lambda v, _pc=pc: v.reciprocal(out=rstdc[:, _pc],
                                                   in_=rstdc[:, _pc]), deps=[sr])
        rst = kb.sig("v")
        x1deps = [rst] + ([x1tok_rd[0]] if x1tok_rd[0] else [])
        kb.ts("v", x1tok[:, :], zsl, mucol[:, pc],
              ALU.subtract, s2=rstdc[:, pc], op1=ALU.mult, deps=x1deps)
        x1t = kb.sig("v")
        col_rd[tp] = x1t
        zrow_rd[tp] = x1t
        x1_st.append(kb.dma("x1st", x1_d.ap()[tsl, :],
                            x1tok[:, :], deps=[x1t]))
        x1tok_rd[0] = x1_st[-1]

    if PHASE <= 4:
        return finish_early()
    x1T_rd = [None, None]

    def x1T_load(tc):
        buf = tc % 2
        deps = [x1_st[4 * tc + 3]]
        if x1T_rd[buf] is not None:
            deps = deps + [x1T_rd[buf]]
        toks = []
        for dc in range(DC):
            dst = x1T[:, buf * DC * 512 + dc * 512:buf * DC * 512 + (dc + 1) * 512]
            src = x1_d.ap()[tc * 512:(tc + 1) * 512, dc * 128:(dc + 1) * 128]
            toks.append(kb.dma("x1t%d" % buf, dst, src,
                               deps=deps if dc == 0 else (), transpose=True))
        return toks

    for tc in range(4):
        buf = tc % 2
        ltoks = x1T_load(tc)
        last_rm = None
        for t4 in range(4):
            ti = tc * 4 + t4
            tp = ti % 2
            bk = B[6] if tp == 0 else B[5]
            psl = slice(0, E)
            bk.mm(psl, ones1[0:1, 0:128], bgrow[0:1, :], start=True, stop=False,
                  deps=(ltoks + [wgg_t, bgr_t]) if t4 == 0 else ())
            for dc in range(DC):
                bk.mm(psl,
                      x1T[:, buf * DC * 512 + dc * 512 + t4 * 128:
                          buf * DC * 512 + dc * 512 + (t4 + 1) * 128],
                      wg16b[:, dc * E:(dc + 1) * E],
                      start=False, stop=(dc == DC - 1))
            rm = kb.sig("t")
            last_rm = rm
            kb.cp("s", sctok[:, tp * E:(tp + 1) * E], bk.h[:, 0:E], deps=[rm])
            bk.done(kb.sig("s"))
            sck = kb.sig("s")
            kb.raw("v", lambda v, _tp=tp: v.max(
                out=m2col[:, _tp * 8:(_tp + 1) * 8],
                in_=sctok[:, _tp * E:(_tp + 1) * E]), deps=[sck])
            m8 = kb.sig("v")
            kb.ts("v", msk_tok[:, tp * 128:tp * 128 + E],
                  sctok[:, tp * E:(tp + 1) * E],
                  m2col[:, tp * 8 + 1:tp * 8 + 2], ALU.is_ge, deps=[m8])
            mskt = kb.sig("v")
            BT.tr(slice(640 + tp * 128, 768 + tp * 128),
                  msk_tok[:, tp * 128:(tp + 1) * 128], Ibf[:],
                  deps=[mskt, g_init])
            mtm = kb.sig("t")
            kb.cp("s", maskT[0:E, ti * 128:(ti + 1) * 128],
                  BT.h[0:E, 640 + tp * 128:640 + tp * 128 + 128], deps=[mtm])
            BT.done(kb.sig("s"))
        x1T_rd[buf] = last_rm
    mask_done = kb.sig("s")
    kb.reduce(nposl[0:E, 0:1], maskT[0:E, :], ALU.add, deps=[mask_done])
    npt = kb.sig("v")
    ag1 = kb.dma("ccb", cin_d.ap(), nposl[0:E, 0:1], deps=[npt])
    cc2 = kb.collective("AllGather", ALU.bypass, [[0, 1, 2, 3, 4, 5, 6, 7]],
                        cin_d, cout_d, deps=[ag1])
    ag2 = kb.dma("ccb", cnt8[:], cout_d.ap(), deps=[cc2])
    kb.ts("v", nzl[:], cnt8[:], -1.0, ALU.mult, s2=float(TOK), op1=ALU.add,
          deps=[ag2])
    nzt = kb.sig("v")
    B[6].mm(slice(0, E), bm8c[0:8, 0:1], cnt8[0:8, 0:E], deps=[ag2, bmm_t],
            rows=slice(0, 1))
    B[6].mm(slice(16, 16 + E), pm8c[0:8, 0:1], nzl[0:8, 0:E], deps=[nzt, pm_t],
            rows=slice(0, 1))
    cmt = kb.sig("t")
    kb.cp("v", cntrow[0:1, :], B[6].h[0:1, 0:32], deps=[cmt])
    B[6].done(kb.sig("v"))
    crt = kb.sig("v")
    cb1 = kb.dma("misc", crd.ap(), cntrow[0:1, :], deps=[crt])
    cb2 = kb.dma("misc", padc[0:E, 0:1], bass.AP(crd, 0, [[1, E], [1, 1]]),
                 deps=[cb1])
    cb3 = kb.dma("misc", offsc[0:E, 0:1], bass.AP(crd, 16, [[1, E], [1, 1]]),
                 deps=[cb1])
    kb.ts("v", padc[0:E, 0:1], padc[0:E, 0:1], -1.0, ALU.mult, s2=float(TOK),
          op1=ALU.add, deps=[cb2])
    padt = kb.sig("v")
    kb.raw("g", lambda g: g.memset(tmp16[0:E, :], 0.0))
    z16t = kb.sig("g")
    kb.ts("v", zT16[0:E, :], maskT[0:E, :], -1.0, ALU.mult, s2=1.0, op1=ALU.add,
          deps=[mask_done])
    ztt = kb.sig("v")
    kb.raw("v", lambda v: v.tensor_tensor_scan(
        out=zT16[0:E, :], data0=zT16[0:E, :], data1=tmp16[0:E, :], initial=0.0,
        op0=ALU.add, op1=ALU.add), deps=[ztt, z16t])
    zst = kb.sig("v")
    kb.ts("v", zT16[0:E, :], zT16[0:E, :], offsc[0:E, 0:1], ALU.add,
          s2=padc[0:E, 0:1], op1=ALU.is_le, deps=[zst, padt, cb3])
    condt = kb.sig("v")
    kb.ts("v", tmp16[0:E, :], maskT[0:E, :], -1.0, ALU.mult, s2=1.0, op1=ALU.add,
          deps=[mask_done, zst])
    tmt = kb.sig("v")
    kb.tt("v", tmp16[0:E, :], tmp16[0:E, :], zT16[0:E, :], ALU.mult,
          deps=[tmt, condt])
    tzt = kb.sig("v")
    kb.tt("v", selT[0:E, :], maskT[0:E, :], tmp16[0:E, :], ALU.add, deps=[tzt])
    sel_done = kb.sig("v")

    if PHASE <= 5:
        return finish_early()
    B[6].mm(S512, ones1[0:1, 0:128], grow_[0:1, 0:512], deps=[gr_t, sel_done])
    gm1 = kb.sig("t")
    kb.cp("s", gamb[:, 0:512], B[6].h[:, S512], deps=[gm1])
    B[6].done(kb.sig("s"))
    B[6].mm(S512, ones1[0:1, 0:128], grow_[0:1, 512:1024], deps=[gr_t])
    gm2 = kb.sig("t")
    kb.cp("s", gamb[:, 512:1024], B[6].h[:, S512], deps=[gm2])
    B[6].done(kb.sig("s"))
    B[6].mm(S512, ones1[0:1, 0:128], brow_[0:1, 0:512], deps=[br_t])
    bm1 = kb.sig("t")
    kb.cp("s", betb[:, 0:512], B[6].h[:, S512], deps=[bm1])
    B[6].done(kb.sig("s"))
    B[6].mm(S512, ones1[0:1, 0:128], brow_[0:1, 512:1024], deps=[br_t])
    bm2 = kb.sig("t")
    kb.cp("s", betb[:, 512:1024], B[6].h[:, S512], deps=[bm2])
    B[6].done(kb.sig("s"))
    gb_done = kb.sig("s")

    if debug_taps:
        dbg_sel = nc.dram_tensor("dbg_selT", [E, TOK], BF16, kind="ExternalOutput")
        dsl = kb.dma("misc", dbg_sel.ap(), selT[0:E, :], deps=[sel_done])
        dbg_x1 = nc.dram_tensor("dbg_x1", [TOK, D], BF16, kind="ExternalOutput")
        dx1 = kb.dma("misc", dbg_x1.ap(), x1_d.ap(), deps=x1_st)
        dbg_og = nc.dram_tensor("dbg_ogT", [128, TOK], BF16, kind="ExternalOutput")
        dog = kb.dma("misc", dbg_og.ap(), ogT[:], deps=[og_all])
        kb.wait("y", dsl)
        kb.wait("y", dx1)
        kb.wait("y", dog)

    out_toks = []
    hb_ready = [sel_done]
    x1ld_rd = [None, None]
    zrow_rd2 = [None, None]
    col_rd2 = [None, None]
    outf_rd = [None, None]
    w1_ld = {}

    def w1_issue(i, deps):
        tc4, e = divmod(i, E)
        wbuf = i % 3
        w1_ld[i] = kb.dma("w1s%d" % (wbuf % 2),
                          w1cur[:, wbuf * 2048:(wbuf + 1) * 2048],
                          w1bf_d.ap()[e], deps=list(deps) + [w1pre[e]])

    for _i in range(3):
        w1_issue(_i, [])
    w1_mm_done = {}
    pending_ltoks = {0: None}
    for tc in range(4):
        buf = tc % 2
        tcs = slice(tc * 512, (tc + 1) * 512)
        ltoks = pending_ltoks.get(tc) or x1T_load(tc)
        h_toks = []
        for e in range(E):
            i = tc * E + e
            wbuf = i % 3
            ep = e % 2
            bks = (B[0], B[1]) if ep == 0 else (B[2], B[3])
            selbk = B[6] if ep == 0 else B[5]
            selbk.mm(S512, IdE[0:E, e * 128:(e + 1) * 128], selT[0:E, tcs],
                     deps=[sel_done, msk_done] + (hb_ready if e < 2 else []))
            sbm = kb.sig("t")
            wdeps = [w1_ld[i]]
            for half in range(2):
                bk = bks[half]
                for dc in range(DC):
                    bk.mm(S512,
                          w1cur[:, wbuf * 2048 + dc * FP + half * 128:
                                wbuf * 2048 + dc * FP + half * 128 + 128],
                          x1T[:, buf * DC * 512 + dc * 512:
                              buf * DC * 512 + (dc + 1) * 512],
                          start=(dc == 0), stop=(dc == DC - 1),
                          deps=(ltoks + wdeps + hb_ready) if dc == 0 else ())
            m1 = kb.sig("t")
            w1_mm_done[(tc, e)] = m1
            if i + 3 < 64:
                w1_issue(i + 3, [m1])
            for half in range(2):
                kb.act(hbuf[:, (e * 2 + half) * 512:(e * 2 + half + 1) * 512],
                       bks[half].h[:, S512], AF.Gelu,
                       bias=b1c[:, e * 2 + half:e * 2 + half + 1],
                       deps=[m1, b1_t] + hb_ready)
            gl = kb.sig("s")
            bks[0].done(gl)
            bks[1].done(gl)
            for half in range(2):
                kb.tt("v", hbuf[:, (e * 2 + half) * 512:(e * 2 + half + 1) * 512],
                      hbuf[:, (e * 2 + half) * 512:(e * 2 + half + 1) * 512],
                      selbk.h[:, S512], ALU.mult, deps=[sbm, gl])
            selbk.done(kb.sig("v"))
            h_toks.append(kb.sig("v"))
        x1T_rd[buf] = w1_mm_done[(tc, E - 1)]
        if tc + 1 < 4:
            pending_ltoks[tc + 1] = x1T_load(tc + 1)
        stg_sigs = []
        for pair in range(2):
            tis = (2 * pair, 2 * pair + 1)
            bks = {(ti4, dh): B[2 + 2 * (ti4 - tis[0]) + dh]
                   for ti4 in tis for dh in range(2)}
            for ti4 in tis:
                gtile = tc * 4 + ti4
                for dh in range(2):
                    bks[(ti4, dh)].mm(
                        S512, selT[0:E, gtile * 128:(gtile + 1) * 128],
                        b2r[0:E, dh * 512:(dh + 1) * 512], start=True,
                        stop=False, deps=[sel_done, b2_t] + h_toks)
            for e in range(E):
                for fpc in range(2):
                    for ti4 in tis:
                        for dh in range(2):
                            bks[(ti4, dh)].mm(
                                S512,
                                hbuf[:, (e * 2 + fpc) * 512 + ti4 * 128:
                                     (e * 2 + fpc) * 512 + (ti4 + 1) * 128],
                                w2b[:, e * 2 * D + fpc * D + dh * 512:
                                    e * 2 * D + fpc * D + (dh + 1) * 512],
                                start=False, stop=(e == E - 1 and fpc == 1),
                                deps=[w2_toks[e]]
                                if (e == 0 and fpc == 0 and ti4 == tis[0]
                                    and dh == 0) else ())
            m2m = kb.sig("t")
            for ti4 in tis:
                for dh in range(2):
                    kb.cp("s", ostg[:, ti4 * D + dh * 512:ti4 * D + (dh + 1) * 512],
                          bks[(ti4, dh)].h[:, S512], deps=[m2m])
                    sg = kb.sig("s")
                    bks[(ti4, dh)].done(sg)
                    stg_sigs.append(sg)
        hb_ready = stg_sigs
        for ti4 in range(4):
            gtile = tc * 4 + ti4
            tp = gtile % 2
            pc = slice(tp, tp + 1)
            xl = kb.dma("x1ld", x1ld[:, :],
                        x1_d.ap()[gtile * 128:(gtile + 1) * 128, :],
                        deps=[x1_st[gtile]] + ([x1ld_rd[0]] if x1ld_rd[0] else []))
            osl = ostg[:, ti4 * D:(ti4 + 1) * D]
            zsl = zrow[:, tp * D:(tp + 1) * D]
            zdeps = stg_sigs + [xl] + ([zrow_rd2[tp]] if zrow_rd2[tp] else [])
            kb.tt("g", zsl, osl, x1ld[:, :], ALU.add,
                  deps=zdeps)
            z2t = kb.sig("g")
            cdeps = [z2t] + ([col_rd2[tp]] if col_rd2[tp] else [])
            kb.reduce(sumc[:, pc], zsl, ALU.add, deps=cdeps)
            s1 = kb.sig("v")
            kb.act(outf[:, :], zsl, AF.Square, accum=sqc[:, pc],
                   deps=cdeps + ([outf_rd[0]] if outf_rd[0] else []))
            s2_ = kb.sig("s")
            kb.ts("v", mucol[:, pc], sumc[:, pc], 1.0 / D, ALU.mult, deps=[s1])
            kb.ts("v", varcol[:, pc], sqc[:, pc], 1.0 / D, ALU.mult, deps=[s2_])
            vready = kb.sig("v")
            kb.stt("v", varcol[:, pc], mucol[:, pc], mucol[:, pc],
                   varcol[:, pc], ALU.mult, ALU.subtract, deps=[vready])
            nv = kb.sig("v")
            kb.act(rstdc[:, pc], varcol[:, pc], AF.Sqrt, scale=-1.0, bias=LNEPS,
                   deps=[nv])
            sr = kb.sig("s")
            kb.raw("v", lambda v, _pc=pc: v.reciprocal(out=rstdc[:, _pc],
                                                       in_=rstdc[:, _pc]),
                   deps=[sr])
            rs = kb.sig("v")
            t2sl = t2b[:, tp * D:(tp + 1) * D]
            kb.ts("v", t2sl, zsl, mucol[:, pc], ALU.subtract,
                  s2=rstdc[:, pc], op1=ALU.mult, deps=[rs])
            t2t = kb.sig("v")
            kb.tt("v", t2sl, t2sl, gamb[:, :], ALU.mult, deps=[t2t, gb_done])
            t3t = kb.sig("v")
            kb.tt("g", t2sl, t2sl, betb[:, :], ALU.add, deps=[t3t])
            t4t = kb.sig("g")
            z3sl = t2sl
            kb.tt("g", z3sl, t2sl, x1ld[:, :], ALU.add,
                  deps=[t4t])
            z3t = kb.sig("g")
            x1ld_rd[0] = z3t
            zrow_rd2[tp] = z3t
            cdeps2 = [z3t]
            kb.reduce(sumc[:, pc], z3sl, ALU.add, deps=cdeps2)
            s3 = kb.sig("v")
            kb.act(outf[:, :], z3sl, AF.Square, accum=sqc[:, pc], deps=cdeps2)
            s4 = kb.sig("s")
            kb.ts("v", mucol[:, pc], sumc[:, pc], 1.0 / D, ALU.mult, deps=[s3])
            kb.ts("v", varcol[:, pc], sqc[:, pc], 1.0 / D, ALU.mult, deps=[s4])
            vready2 = kb.sig("v")
            kb.stt("v", varcol[:, pc], mucol[:, pc], mucol[:, pc],
                   varcol[:, pc], ALU.mult, ALU.subtract, deps=[vready2])
            nv2 = kb.sig("v")
            kb.act(rstdc[:, pc], varcol[:, pc], AF.Sqrt, scale=-1.0,
                   bias=LNEPS, deps=[nv2])
            sr2 = kb.sig("s")
            kb.raw("v", lambda v, _pc=pc: v.reciprocal(out=rstdc[:, _pc],
                                                       in_=rstdc[:, _pc]),
                   deps=[sr2])
            rs2 = kb.sig("v")
            ofsl = outf[:, :]
            kb.ts("v", ofsl, z3sl, mucol[:, pc], ALU.subtract,
                  s2=rstdc[:, pc], op1=ALU.mult,
                  deps=[rs2] + ([outf_rd[0]] if outf_rd[0] else []))
            oft = kb.sig("v")
            col_rd2[tp] = oft
            out_toks.append(kb.dma("out",
                                   out_h.ap()[gtile * 128:(gtile + 1) * 128, :],
                                   ofsl, deps=[oft]))
            outf_rd[0] = out_toks[-1]

    for tok in out_toks:
        kb.wait("y", tok)
        kb.wait("g", tok)

    return kb.build()


_CACHE = {}


def kernel(**inputs):
    if "nc" not in _CACHE:
        _CACHE["nc"] = build_full(debug_taps=bool(os.environ.get("KNL_TAPS")))
    nc = _CACHE["nc"]
    in_maps = make_in_maps(inputs)
    res = run_bass_kernel_spmd(nc, in_maps, core_ids=list(range(8)))
    shards = [np.asarray(res.results[r]["out"], np.float32) for r in range(8)]
    out = np.stack(shards, 0).reshape(4, 4096, D)
    if os.environ.get("KNL_TAPS"):
        kernel.last_results = res
    return out


def make_in_maps(inputs):
    f32 = lambda a: np.ascontiguousarray(np.asarray(a, np.float32))
    x = f32(inputs["L0_attn_x"])
    Wa = f32(inputs["L0_attn_Wa"]); Wb = f32(inputs["L0_attn_Wb"])
    ba = f32(inputs["L0_attn_ba"]); bb = f32(inputs["L0_attn_bb"])
    b1 = f32(inputs["L0_ffn_b1"])
    shared = dict(
        Wq=f32(inputs["L0_attn_Wq"]), Wk=f32(inputs["L0_attn_Wk"]),
        Wv=f32(inputs["L0_attn_Wv"]), Wg=f32(inputs["L0_attn_Wg"]),
        Wab=np.ascontiguousarray(np.concatenate([Wa, Wb], 1)),
        Wo=f32(inputs["L0_attn_Wo"]),
        bqkvg=np.ascontiguousarray(np.stack(
            [f32(inputs["L0_attn_bq"]), f32(inputs["L0_attn_bk"]),
             f32(inputs["L0_attn_bv"]), f32(inputs["L0_attn_bg"])], 1)),
        bab=np.array([[ba[0]], [bb[0]]], np.float32),
        bo_row=np.ascontiguousarray(f32(inputs["L0_attn_bo"])[None, :]),
        wg16=f32(inputs["L0_ffn_wg"]),
        bg16=np.ascontiguousarray(f32(inputs["L0_ffn_bg"]).reshape(E, 1)),
        w1=f32(inputs["L0_ffn_w1"]),
        b1c=np.ascontiguousarray(
            b1.reshape(E, 2, 128).transpose(2, 0, 1).reshape(128, 32)),
        w2=f32(inputs["L0_ffn_w2"]),
        b2r=f32(inputs["L0_ffn_b2"]),
        gamma_row=np.ascontiguousarray(f32(inputs["L0_ffn_gamma"])[None, :]),
        beta_row=np.ascontiguousarray(f32(inputs["L0_ffn_beta"])[None, :]),
    )
    maps = []
    for r in range(8):
        b, h = r // 2, r % 2
        sl = slice(h * TOK, (h + 1) * TOK)
        m = dict(shared)
        m["xT"] = np.ascontiguousarray(x[b].T[:, sl])
        m["xrows"] = np.ascontiguousarray(x[b, sl])
        m["hmask"] = np.full((128, 1), 1.0 if h == 0 else 0.0, np.float32)
        m["eps_role"] = np.full((128, 1), 1.0 if h == 1 else 0.0, np.float32)
        pm = np.zeros((8, 1), np.float32)
        if h == 1:
            pm[2 * b, 0] = 1.0
        m["pm8"] = pm
        bm = np.zeros((8, 1), np.float32)
        bm[2 * b, 0] = 1.0
        bm[2 * b + 1, 0] = 1.0
        m["bm8"] = bm
        maps.append(m)
    return maps


# version 16 (final)
# speedup vs baseline: 1.0187x; 1.0187x over previous
import os
import numpy as np
import concourse.bass as bass
import concourse.mybir as mybir
from concourse.bass_utils import run_bass_kernel_spmd

F32 = mybir.dt.float32
BF16 = mybir.dt.bfloat16
AF = mybir.ActivationFunctionType
ALU = mybir.AluOpType
AX = mybir.AxisListType

TOK = 2048
D = 1024
C = 128
E = 16
FP = 256
NT = TOK // 128
DC = D // 128
NLEV = 4
LNEPS = 1e-5


class KB:
    ENG = ("t", "v", "s", "g", "y")

    LANES = {
        "ld0": ("g", True), "ld0y": ("y", True), "w1pre": ("g", True),
        "w2": ("g", True), "x1st": ("y", False), "misc": ("y", True),
        "ccb": ("g", True), "out": ("y", True),
        "w1s0": ("g", False), "w1s1": ("g", False),
        "x1t0": ("y", False), "x1t1": ("y", False),
        "xrow": ("y", False), "x1ld": ("y", False),
    }

    def __init__(self):
        self.nc = bass.Bass("TRN2", target_bir_lowering=False,
                            detect_race_conditions=not bool(
                                os.environ.get("KNL_NORACE")))
        self.prog = []
        self.esem = {e: self.nc.alloc_semaphore(name="sem_" + e) for e in self.ENG}
        self.cnt = {e: 0 for e in self.ENG}
        self.dsem = {q: self.nc.alloc_semaphore(name="sem_" + q) for q in self.LANES}
        self.dcnt = {q: 0 for q in self.LANES}
        self.ccsem = self.nc.alloc_semaphore(name="sem_cc")
        self.cccnt = 0
        self.hiwater = {e: {} for e in self.ENG}
        self.last_op = {e: None for e in self.ENG}

    def raw(self, e, fn, deps=()):
        for tok in deps:
            self.wait(e, tok)
        self.prog.append([e, fn, []])
        self.last_op[e] = len(self.prog) - 1

    def wait(self, e, tok):
        if tok is None:
            return
        if len(tok) == 3:
            lane, sem, val = tok
            if self.LANES[lane][1]:
                val = self.dcnt[lane]
        else:
            sem, val = tok
        hw = self.hiwater[e]
        k = sem.name
        if hw.get(k, -1) >= val:
            return
        hw[k] = val
        self.prog.append([e, lambda h: h.wait_ge(sem, val), None])

    def sig(self, e):
        self.cnt[e] += 1
        v = self.cnt[e]
        sem = self.esem[e]
        li = self.last_op[e]
        if li is not None and self.prog[li][2] is not None:
            incs = self.prog[li][2]
            for j, (sm, n) in enumerate(incs):
                if sm is sem:
                    incs[j] = (sm, n + 1)
                    break
            else:
                incs.append((sem, 1))
        else:
            self.prog.append([e, lambda h: h.nop().then_inc(sem, 1), None])
        return (sem, v)

    def dma(self, lane, out_ap, in_ap, deps=(), transpose=False):
        eng = self.LANES[lane][0]
        for tok in deps:
            self.wait(eng, tok)
        self.dcnt[lane] += 16
        v = self.dcnt[lane]
        sem = self.dsem[lane]
        if transpose:
            self.prog.append([eng, lambda h: h.dma_start_transpose(
                out=out_ap, in_=in_ap).then_inc(sem, 16), None])
        else:
            self.prog.append([eng, lambda h: h.dma_start(
                out=out_ap, in_=in_ap).then_inc(sem, 16), None])
        return (lane, sem, v)

    def mm(self, psum_ap, lhsT_ap, rhs_ap, start=True, stop=True, deps=()):
        self.raw("t", lambda t: t.matmul(psum_ap, lhsT_ap, rhs_ap,
                                         start=start, stop=stop), deps)

    def tr(self, psum_ap, in_ap, ident_ap, deps=()):
        self.raw("t", lambda t: t.transpose(psum_ap, in_ap, ident_ap), deps)

    def cp(self, e, out, in_, deps=()):
        if e == "s":
            self.raw(e, lambda h: h.copy(out=out, in_=in_), deps)
        else:
            self.raw(e, lambda h: h.tensor_copy(out=out, in_=in_), deps)

    def act(self, out, in_, func, bias=None, scale=None, accum=None, deps=()):
        kw = {}
        if bias is not None:
            kw["bias"] = bias
        if scale is not None:
            kw["scale"] = scale
        if accum is not None:
            kw["accum_out"] = accum
        self.raw("s", lambda h: h.activation(out=out, in_=in_, func=func, **kw),
                 deps)

    def tt(self, e, out, a, b, op, deps=()):
        self.raw(e, lambda h: h.tensor_tensor(out=out, in0=a, in1=b, op=op), deps)

    def ts(self, e, out, in0, s1, op0, s2=None, op1=None, deps=()):
        if op1 is None:
            self.raw(e, lambda h: h.tensor_scalar(out=out, in0=in0, scalar1=s1,
                                                  scalar2=None, op0=op0), deps)
        else:
            self.raw(e, lambda h: h.tensor_scalar(out=out, in0=in0, scalar1=s1,
                                                  scalar2=s2, op0=op0, op1=op1),
                     deps)

    def stt(self, e, out, in0, scalar, in1, op0, op1, deps=()):
        self.raw(e, lambda h: h.scalar_tensor_tensor(
            out=out, in0=in0, scalar=scalar, in1=in1, op0=op0, op1=op1), deps)

    def reduce(self, out, in_, op, deps=()):
        self.raw("v", lambda v: v.tensor_reduce(out=out, in_=in_, axis=AX.X,
                                                op=op), deps)

    def collective(self, kind, op, groups, in_dram, out_dram, deps=()):
        self.cccnt += 1
        v = self.cccnt
        sem = self.ccsem

        def fn(h):
            return h.collective_compute(kind, op, replica_groups=groups,
                                        ins=[in_dram.ap().opt()],
                                        outs=[out_dram.ap().opt()]).then_inc(sem, 1)
        self.raw("g", fn, deps)
        self.prog[self.last_op["g"]][2] = None
        self.last_op["g"] = None
        return (sem, v)

    def build(self):
        nc, prog = self.nc, self.prog

        def run(h, key):
            for ent in prog:
                if ent[0] != key:
                    continue
                inst = ent[1](h)
                if ent[2]:
                    for sem, n in ent[2]:
                        inst = inst.then_inc(sem, n)

        with nc.Block() as block:
            @block.tensor
            def _(t):
                run(t, "t")

            @block.vector
            def _(v):
                run(v, "v")

            @block.scalar
            def _(s):
                run(s, "s")

            @block.gpsimd
            def _(g):
                run(g, "g")

            @block.sync
            def _(y):
                run(y, "y")
        return nc


class Bank:

    def __init__(self, kb, handle):
        self.kb = kb
        self.h = handle
        self.read_tok = None

    def wdeps(self, deps):
        d = list(deps)
        if self.read_tok is not None:
            d.append(self.read_tok)
        return d

    def mm(self, sl, lhsT, rhs, start=True, stop=True, deps=(), rows=None):
        ap = self.h[:, sl] if rows is None else self.h[rows, sl]
        self.kb.mm(ap, lhsT, rhs, start=start, stop=stop,
                   deps=self.wdeps(deps) if start else deps)

    def tr(self, sl, in_ap, ident, deps=(), rows=None):
        ap = self.h[:, sl] if rows is None else self.h[rows, sl]
        self.kb.tr(ap, in_ap, ident, deps=self.wdeps(deps))

    def done(self, tok):
        self.read_tok = tok


def build_full(debug_taps=False):
    PHASE = int(os.environ.get("KNL_PHASE", "9"))
    kb = KB()
    nc = kb.nc
    TAPS = {}

    def finish_early():
        for e in kb.ENG:
            if kb.cnt[e]:
                kb.wait("g", (kb.esem[e], kb.cnt[e]))
                kb.wait("y", (kb.esem[e], kb.cnt[e]))
        for q in kb.LANES:
            if kb.dcnt[q]:
                kb.wait("g", (q, kb.dsem[q], kb.dcnt[q]))
                kb.wait("y", (q, kb.dsem[q], kb.dcnt[q]))
        if os.environ.get("KNL_TAPS"):
            for nm, ap in list(TAPS.items()):
                t = nc.dram_tensor("tap_" + nm, list(ap.shape), ap.dtype,
                                   kind="ExternalOutput")
                kb.dma("out", t.ap(), ap)
        kb.wait("y", ("out", kb.dsem["out"], kb.dcnt["out"]))
        kb.wait("g", ("out", kb.dsem["out"], kb.dcnt["out"]))
        return kb.build()

    din = lambda nm, shp: nc.dram_tensor(nm, shp, F32, kind="ExternalInput")
    xT_h = din("xT", [D, TOK])
    xrows_h = din("xrows", [TOK, D])
    Wq_h = din("Wq", [D, C]); Wk_h = din("Wk", [D, C])
    Wv_h = din("Wv", [D, C]); Wg_h = din("Wg", [D, C])
    Wab_h = din("Wab", [D, 2]); Wo_h = din("Wo", [C, D])
    bqkvg_h = din("bqkvg", [C, 4]); bab_h = din("bab", [2, 1])
    bo_h = din("bo_row", [1, D])
    wg16_h = din("wg16", [D, E]); bg16_h = din("bg16", [E, 1])
    w1_h = din("w1", [E, D, FP]); b1c_h = din("b1c", [128, 32])
    w2_h = din("w2", [E, FP, D]); b2_h = din("b2r", [E, D])
    gam_h = din("gamma_row", [1, D]); bet_h = din("beta_row", [1, D])
    hm_h = din("hmask", [128, 1]); ep_h = din("eps_role", [128, 1])
    pm8_h = din("pm8", [8, 1]); bm8_h = din("bm8", [8, 1])
    out_h = nc.dram_tensor("out", [TOK, D], F32, kind="ExternalOutput")

    cb_d = nc.dram_tensor("cbounce", [2, TOK], F32)
    x1_d = nc.dram_tensor("x1rows", [TOK, D], BF16)
    zin_d = nc.dram_tensor("z_ar_in", [C, C], F32)
    zout_d = nc.dram_tensor("z_ar_out", [C, C], F32)
    cin_d = nc.dram_tensor("cnt_ag_in", [E, 1], F32)
    cout_d = nc.dram_tensor("cnt_ag_out", [8, E], F32)
    crd = nc.dram_tensor("cntrow_d", [1, 32], F32)
    w1bf_d = nc.dram_tensor("w1bf", [E, 128, DC * FP], BF16)

    sb = lambda nm, shp, dt=F32: nc.alloc_sbuf_tensor(nm, shp, dt)

    W2R = sb("W2R", [128, 32768], BF16)
    HR = sb("HR", [128, 16384], BF16)
    OR_ = sb("OR", [128, 8192], BF16)
    W2Rf = W2R.bitcast(F32)
    HRf = HR.bitcast(F32)
    ORf = OR_.bitcast(F32)

    w2b = W2R
    xTb = W2R[:, 0:16384]
    KtA = W2R[:, 16384:18432]
    WTA = W2R[:, 18432:20480]
    UeA = W2R[:, 20480:22528]
    A2A = W2R[:, 22528:24576]
    abT = W2Rf[0:2, 12288:14336]
    c_row = W2Rf[0:1, 14336:16384]

    hbuf = HR
    qT = HR[:, 0:2048]
    ktT = HR[:, 2048:4096]
    vT = HR[:, 4096:6144]
    gT = HR[:, 6144:8192]
    ogT = HR[:, 8192:10240]
    tmpP = HR[:, 10240:14336]
    tmpPf = HRf[:, 5120:7168]
    tmp16 = HRf[0:16, 5120:7168]

    _bofs = [0]
    def cb(n=128):
        o = _bofs[0]; _bofs[0] += n
        return tmpP[:, o:o + n]
    gamt = cb(); gamtT = cb(); Pt = cb(); PTt = cb(); P2t = cb(); PT2t = cb()
    Rt = cb(); RTt = cb(); GRT = cb(); Bk = cb(); Bv = cb(); UTt = cb()
    WZs = cb(); IgL = cb(); MqT = cb(); T4n = cb(); gbc = cb(); scrb = cb()
    _fofs = [5120 + 2304 // 2]
    def cf(n=128):
        o = _fofs[0]; _fofs[0] += n
        return HRf[:, o:o + n]
    dm = cf(); scr = cf(); e1t = cf(); OA = cf(); Ot1 = cf()

    ostg = ORf[:, 0:4096]
    scoresT = ORf[0:16, 0:2048]
    zT16 = ORf[0:16, 2048:4096]
    nrm = ORf[0:1, 0:2048]
    nrmb = OR_[0:1, 4096:6144]
    crb = OR_[0:1, 6144:8192]
    growR = ORf[0:1, 0:2048]
    growRb = OR_[0:1, 4096:6144]
    iotE = ORf[0:16, 0:2048]

    wqb = sb("wqb", [128, DC * C], BF16)
    wkb = sb("wkb", [128, DC * C], BF16)
    wvb = sb("wvb", [128, DC * C], BF16)
    wgb = sb("wgb", [128, DC * C], BF16)
    wabb = sb("wabb", [128, DC * 2], BF16)
    wob = sb("wob", [128, D], BF16)
    wg16b = sb("wg16b", [128, DC * E], BF16)
    bqkvg = sb("bqkvgS", [C, 4], F32)
    bab = sb("babS", [2, 1], F32)
    borow = sb("borowS", [1, D], BF16)
    bg16c = sb("bg16S", [E, 1], F32)
    b1c = sb("b1cS", [128, 32], F32)
    b2r = sb("b2rS", [E, D], BF16)
    grow_ = sb("gamrowB", [1, D], BF16)
    brow_ = sb("betrowB", [1, D], BF16)
    gamb = sb("gammaB", [128, D], BF16)
    betb = sb("betaB", [128, D], BF16)
    selT = sb("selT", [E, TOK], BF16)
    maskT = sb("maskT", [E, TOK], BF16)
    hmcol = sb("hmcol", [128, 1], F32)
    epcol = sb("epcol", [128, 1], F32)
    pm8c = sb("pm8c", [8, 1], F32)
    bm8c = sb("bm8c", [8, 1], F32)
    Ibf = sb("Ibf", [128, 128], BF16)
    NEGSL = sb("NEGSL", [128, 128], BF16)
    TRIU = sb("TRIU", [128, 128], BF16)
    NTRIU = sb("NTRIU", [128, 128], BF16)
    iot = sb("iot", [128, 128], F32)
    If32 = sb("If32", [128, 128], F32)
    ones1 = sb("ones1", [1, 512], BF16)
    ones1f = sb("ones1f", [1, 128], F32)
    onesc = sb("onesc", [128, 1], BF16)
    xrow_t = sb("xrow_t", [128, D], F32)
    zrow = sb("zrow", [128, 2 * D], F32)
    x1tok = sb("x1tok", [128, D], BF16)
    x1ld = sb("x1ld", [128, D], BF16)
    outf = sb("outf", [128, D], F32)
    t2b = sb("t2b", [128, 2 * D], BF16)
    sumc = sb("sumc", [128, 2], F32)
    sqc = sb("sqc", [128, 2], F32)
    mucol = sb("mucol", [128, 2], F32)
    varcol = sb("varcol", [128, 2], F32)
    rstdc = sb("rstdc", [128, 2], F32)
    epsc = sb("epsc", [128, 1], F32)
    nc.const_aps.register(F32, LNEPS, epsc[:, 0:1])
    k2 = sb("k2scr", [128, 512], BF16)
    cbx = sb("cbx", [128, 2 * NT], F32)
    prow = sb("prow", [1, 2 * NT], F32)
    pebf = sb("pebf", [1, 2 * NT], BF16)
    gldiff = sb("gldiff", [1, NT], F32)
    gamrow = sb("gamrow", [1, NT], F32)
    gamcol = sb("gamcol", [128, NT], F32)
    ecol = sb("ecol", [128, NT], F32)
    gLbc = sb("gLbc", [128, NT], F32)
    IdE = sb("IdE", [E, E * 128], BF16)
    Zc = sb("Zc", [128, 128], BF16)
    ZTc = sb("ZTc", [128, 128], BF16)
    Zf32 = sb("Zf32", [128, 128], F32)
    sctok = sb("sctok", [128, 2 * E], F32)
    m2col = sb("m2col", [128, 2 * 8], F32)
    msk_tok = sb("msk_tok", [128, 2 * 128], BF16)
    sc_stage_f = sb("sc_stage_f", [128, 128], F32)
    bgrow = sb("bgrow", [1, E], BF16)
    nposl = sb("nposl", [E, 1], F32)
    cnt8 = sb("cnt8", [8, E], F32)
    nzl = sb("nzl", [8, E], F32)
    cntrow = sb("cntrow", [1, 32], F32)
    offsc = sb("offsc", [E, 1], F32)
    padc = sb("padc", [E, 1], F32)
    x1T = sb("x1T", [128, 2 * DC * 512], BF16)
    w1cur = sb("w1cur", [128, 3 * 2048], BF16)
    w1stg = OR_[:, 6144:8192]

    B = [Bank(kb, nc.alloc_psum_tensor("ps%d" % i, [128, 512], F32))
         for i in range(7)]
    BT = Bank(kb, nc.alloc_psum_tensor("psT", [128, 1024], BF16))

    S512 = slice(0, 512)

    xld = [kb.dma("ld0", xTb[:, dc * TOK:(dc + 1) * TOK],
                  xT_h.ap()[dc * 128:(dc + 1) * 128, :]) for dc in range(DC)]
    wq_t = kb.dma("ld0", wqb[:], bass.AP(Wq_h, 0, [[C, 128], [128 * C, DC], [1, C]]))
    wk_t = kb.dma("ld0", wkb[:], bass.AP(Wk_h, 0, [[C, 128], [128 * C, DC], [1, C]]))
    wv_t = kb.dma("ld0", wvb[:], bass.AP(Wv_h, 0, [[C, 128], [128 * C, DC], [1, C]]))
    wgt_t = kb.dma("ld0", wgb[:], bass.AP(Wg_h, 0, [[C, 128], [128 * C, DC], [1, C]]))
    wab_t = kb.dma("ld0", wabb[:], bass.AP(Wab_h, 0, [[2, 128], [128 * 2, DC], [1, 2]]))
    wo_t = kb.dma("ld0", wob[:], Wo_h.ap())
    bo_t = kb.dma("ld0", borow[:], bo_h.ap())
    bq_t = kb.dma("ld0y", bqkvg[:], bqkvg_h.ap())
    ba_t = kb.dma("ld0y", bab[:], bab_h.ap())
    ep_t = kb.dma("ld0y", epcol[:], ep_h.ap())
    hm_t = kb.dma("ld0y", hmcol[:], hm_h.ap())
    pm_t = kb.dma("ld0y", pm8c[:], pm8_h.ap())
    bmm_t = kb.dma("ld0y", bm8c[:], bm8_h.ap())
    wgg_t = kb.dma("ld0", wg16b[:], bass.AP(wg16_h, 0, [[E, 128], [128 * E, DC], [1, E]]))
    bg_t = kb.dma("ld0y", bg16c[:], bg16_h.ap())
    bgr_t = kb.dma("ld0", bgrow[0:1, :], bass.AP(bg16_h, 0, [[E, 1], [1, E]]))
    b1_t = kb.dma("ld0y", b1c[:], b1c_h.ap())
    b2_t = kb.dma("ld0", b2r[:], b2_h.ap())
    gr_t = kb.dma("ld0", grow_[:], gam_h.ap())
    br_t = kb.dma("ld0", brow_[:], bet_h.ap())
    w1pre = []
    prev = None
    for e in range(E):
        d1 = kb.dma("w1pre", w1stg[:],
                    bass.AP(w1_h, e * D * FP, [[FP, 128], [128 * FP, DC], [1, FP]]),
                    deps=[prev] if prev else ())
        d2 = kb.dma("w1pre", w1bf_d.ap()[e], w1stg[:], deps=[d1])
        w1pre.append(d2)
        prev = d2

    kb.raw("g", lambda g: g.iota(iot[:], pattern=[[-1, 128]], base=0,
                                 channel_multiplier=1,
                                 allow_small_or_imprecise_dtypes=True))
    kb.raw("g", lambda g: g.memset(ones1[:], 1.0))
    kb.raw("g", lambda g: g.memset(ones1f[:], 1.0))
    kb.raw("g", lambda g: g.memset(onesc[:], 1.0))
    kb.raw("g", lambda g: g.memset(outf[0:1, :], 0.0))
    kb.raw("g", lambda g: g.memset(sc_stage_f[:], 0.0))
    kb.raw("g", lambda g: g.memset(msk_tok[:], 0.0))
    kb.raw("g", lambda g: g.memset(Zc[:], 0.0))
    kb.raw("g", lambda g: g.memset(epsc[:], LNEPS))
    kb.raw("g", lambda g: g.iota(iotE[0:E, :], pattern=[[1, E * 128]], base=0,
                                 channel_multiplier=-128,
                                 allow_small_or_imprecise_dtypes=True))
    g_init = kb.sig("g")
    kb.ts("v", Ibf[:], iot[:], 0.0, ALU.is_equal, deps=[g_init])
    kb.ts("v", If32[:], iot[:], 0.0, ALU.is_equal)
    kb.ts("v", NEGSL[:], iot[:], 0.0, ALU.is_gt, s2=-1.0, op1=ALU.mult)
    kb.ts("v", TRIU[:], iot[:], 0.0, ALU.is_le)
    kb.ts("v", NTRIU[:], iot[:], 0.0, ALU.is_le, s2=-1.0, op1=ALU.mult)
    kb.ts("v", zT16[0:E, :], iotE[0:E, :], -0.5, ALU.is_gt, deps=[g_init])
    kb.ts("v", iotE[0:E, :], iotE[0:E, :], 127.5, ALU.is_le)
    negE = kb.sig("v")
    kb.tt("v", IdE[0:E, :], zT16[0:E, :], iotE[0:E, :], ALU.mult, deps=[negE])
    msk_done = kb.sig("v")

    proj_list = [(wqb, 0, qT, wq_t, "copy"), (wkb, 1, ktT, wk_t, "copy"),
                 (wvb, 2, vT, wv_t, "copy"), (wgb, 3, gT, wgt_t, "sigmoid")]
    ev = {}
    for wsb, bi, dst, wtok, mode in proj_list:
        for tc in range(4):
            dep0 = xld + [wtok]
            for dc in range(DC):
                B[tc].mm(S512, wsb[:, dc * C:(dc + 1) * C],
                         xTb[:, dc * TOK + tc * 512: dc * TOK + (tc + 1) * 512],
                         start=(dc == 0), stop=(dc == DC - 1),
                         deps=dep0 if dc == 0 else ())
        mmt = kb.sig("t")
        for tc in range(4):
            sl = slice(tc * 512, (tc + 1) * 512)
            fn = AF.Sigmoid if mode == "sigmoid" else AF.Identity
            kb.act(dst[:, sl], B[tc].h[:, S512], fn, bias=bqkvg[:, bi:bi + 1],
                   deps=[mmt, bq_t])
        etok = kb.sig("s")
        for tc in range(4):
            B[tc].done(etok)
        ev[bi] = etok
    for tc in range(4):
        for dc in range(DC):
            B[4].mm(S512, wabb[:, dc * 2:(dc + 1) * 2],
                    xTb[:, dc * TOK + tc * 512: dc * TOK + (tc + 1) * 512],
                    start=(dc == 0), stop=(dc == DC - 1),
                    deps=(xld + [wab_t]) if dc == 0 else (), rows=slice(0, 2))
        abmm = kb.sig("t")
        kb.act(abT[:, tc * 512:(tc + 1) * 512], B[4].h[0:2, S512], AF.Sigmoid,
               bias=bab[:, 0:1], deps=[abmm, ba_t])
        B[4].done(kb.sig("s"))
    ab_done = kb.sig("s")
    w2_toks = {}
    for e in range(8):
        w2_toks[e] = kb.dma("w2", w2b[:, e * 2 * D:(e + 1) * 2 * D],
                            bass.AP(w2_h, e * FP * D, [[D, 128], [128 * D, 2], [1, D]]),
                            deps=[ab_done])

    k2_rd = None
    for tc in range(4):
        sl = slice(tc * 512, (tc + 1) * 512)
        kb.act(k2[:, :], ktT[:, sl], AF.Square,
               deps=[ev[1], g_init] + ([k2_rd] if k2_rd else []))
        sqt = kb.sig("s")
        B[5].mm(slice(0, 512), onesc[:, 0:1], k2[:, :], deps=[sqt, g_init],
                rows=slice(0, 1))
        mt = kb.sig("t")
        k2_rd = mt
        kb.cp("v", nrm[0:1, sl], B[5].h[0:1, S512], deps=[mt, msk_done])
        B[5].done(kb.sig("v"))
    nrm_done = kb.sig("v")
    kb.act(nrm[0:1, :], nrm[0:1, :], AF.Sqrt, deps=[nrm_done])
    st = kb.sig("s")
    kb.ts("v", nrm[0:1, :], nrm[0:1, :], 1e-12, ALU.max, deps=[st])
    kb.raw("v", lambda v: v.reciprocal(out=nrm[0:1, :], in_=nrm[0:1, :]))
    kb.cp("v", nrmb[0:1, :], nrm[0:1, :])
    rn_done = kb.sig("v")
    for tc in range(4):
        sl = slice(tc * 512, (tc + 1) * 512)
        B[5].mm(S512, ones1[0:1, 0:128], nrmb[0:1, sl], deps=[rn_done])
        bct = kb.sig("t")
        kb.tt("v", ktT[:, sl], ktT[:, sl], B[5].h[:, S512], ALU.mult, deps=[bct])
        B[5].done(kb.sig("v"))
    ktn_done = kb.sig("v")

    kb.act(c_row[0:1, :], abT[0:1, :], AF.Ln, deps=[ab_done])
    lnt = kb.sig("s")
    kb.raw("v", lambda v: v.tensor_tensor_scan(
        out=c_row[0:1, 0:1024], data0=c_row[0:1, 0:1024], data1=outf[0:1, 0:1024],
        initial=0.0, op0=ALU.add, op1=ALU.add), deps=[lnt, g_init])
    sc1 = kb.sig("v")
    kb.raw("v", lambda v: v.tensor_tensor_scan(
        out=c_row[0:1, 1024:2048], data0=c_row[0:1, 1024:2048], data1=outf[0:1, 0:1024],
        initial=c_row[0:1, 1023:1024], op0=ALU.add, op1=ALU.add), deps=[sc1])
    c_done = kb.sig("v")
    bt1 = kb.dma("misc", cb_d.ap()[0:1, :], c_row[0:1, :], deps=[c_done])
    bt2 = kb.dma("misc", cb_d.ap()[1:2, :], abT[1:2, :], deps=[ab_done])
    btr = kb.dma("misc", sc_stage_f[0:32, :],
                 bass.AP(cb_d, 0, [[128, 32], [1, 128]]), deps=[bt1, bt2, g_init])
    kb.wait("t", btr)
    B[6].tr(slice(0, 128), sc_stage_f[:, :], If32[:], deps=[msk_done])
    cbt = kb.sig("t")
    kb.cp("s", cbx[:, 0:32], B[6].h[:, 0:32], deps=[cbt])
    B[6].done(kb.sig("s"))
    bt3 = kb.sig("s")
    kb.cp("v", prow[0:1, 0:2 * NT:2], c_row[0:1, 127:TOK:128], deps=[c_done])
    kb.raw("v", lambda v: v.memset(prow[0:1, 1:2], 0.0))
    kb.cp("v", prow[0:1, 3:2 * NT:2], c_row[0:1, 127:TOK - 128:128])
    prow_done = kb.sig("v")
    kb.tt("v", gldiff[0:1, :], prow[0:1, 0:2 * NT:2], prow[0:1, 1:2 * NT:2],
          ALU.subtract, deps=[prow_done])
    gld = kb.sig("v")
    kb.act(gamrow[0:1, :], gldiff[0:1, :], AF.Exp, deps=[gld])
    B[6].mm(slice(0, 2 * NT), ones1f[0:1, :], prow[0:1, :], deps=[prow_done])
    bcp = kb.sig("t")
    kb.tt("v", gamcol[:], cbx[:, 0:NT], B[6].h[:, 1:2 * NT:2], ALU.subtract,
          deps=[bcp, bt3])
    g1 = kb.sig("v")
    kb.act(gamcol[:], gamcol[:], AF.Exp, deps=[g1])
    kb.tt("v", ecol[:], B[6].h[:, 0:2 * NT:2], cbx[:, 0:NT], ALU.subtract,
          deps=[bcp])
    g2 = kb.sig("v")
    kb.act(ecol[:], ecol[:], AF.Exp, deps=[g2])
    B[6].done(kb.sig("v"))
    kb.cp("v", pebf[0:1, 0:NT], gamrow[0:1, :], deps=[kb.sig("s")])
    grb = kb.sig("v")
    B[6].mm(slice(64, 64 + NT), ones1[0:1, 0:128], pebf[0:1, 0:NT], deps=[grb])
    glm = kb.sig("t")
    kb.cp("s", gLbc[:], B[6].h[:, 64:64 + NT], deps=[glm])
    B[6].done(kb.sig("s"))
    decay_s = kb.sig("s")
    pv_ap = bass.AP(prow, 1, [[2 * NT, 1], [2, NT], [0, 128]])
    kb.cp("v", growR, pv_ap, deps=[prow_done, ktn_done])
    pvr = kb.sig("v")
    kb.tt("v", growR, c_row, growR, ALU.subtract, deps=[pvr, c_done])
    grs = kb.sig("v")
    kb.act(growR, growR, AF.Exp, deps=[grs])
    gre = kb.sig("s")
    kb.cp("v", growRb, growR, deps=[gre, ktn_done])
    growNT_done = kb.sig("v")

    for e in range(12, 14):
        w2_toks[e] = kb.dma("w2", w2b[:, e * 2 * D:(e + 1) * 2 * D],
                            bass.AP(w2_h, e * FP * D, [[D, 128], [128 * D, 2], [1, D]]),
                            deps=[growNT_done, bt3, c_done])

    prep = [ktn_done, decay_s, growNT_done, msk_done, ev[0], ev[2], ev[3]]
    TAPS.update(qT=qT[:], ktT=ktT[:], vT=vT[:], gT=gT[:], cbx=cbx[:],
                prow=prow[0:1, :], gamrow=gamrow[0:1, :], gamcol=gamcol[:],
                ecol=ecol[:], gLbc=gLbc[:], growR=growR, c_row=c_row,
                nrm=nrm)
    if PHASE <= 1:
        return finish_early()

    chA = [g_init]

    def emit_chainA(ch, WTtok, Kttok, Uetok, chA):
        csl = slice(ch * 128, (ch + 1) * 128)
        kb.ts("v", IgL[:], Ibf[:], gLbc[:, ch:ch + 1], ALU.mult, deps=[decay_s])
        igl = kb.sig("v")
        B[2].mm(slice(128, 256), WTA[:, csl], Zc[:], deps=[WTtok] + chA)
        wzm = kb.sig("t")
        kb.ts("v", WZs[:], B[2].h[:, 128:256], gLbc[:, ch:ch + 1], ALU.mult,
              s2=-1.0, op1=ALU.mult, deps=[wzm])
        B[2].done(kb.sig("v"))
        wzs = kb.sig("v")
        B[4].mm(slice(128, 256), IgL[:], Zc[:], start=True, stop=False,
                deps=[igl] + chA)
        B[4].mm(slice(128, 256), KtA[:, csl], WZs[:], start=False, stop=False,
                deps=[wzs, Kttok])
        B[4].mm(slice(128, 256), KtA[:, csl], UeA[:, csl], start=False, stop=True,
                deps=[Uetok])
        znm = kb.sig("t")
        kb.cp("s", Zc[:], B[4].h[:, 128:256], deps=[znm])
        B[4].done(kb.sig("s"))
        return [kb.sig("s")]

    pendA = None
    NCH = int(os.environ.get("KNL_NCH", str(NT)))
    STEP = int(os.environ.get("KNL_STEP", "99"))
    for ch in range(NCH):
        csl = slice(ch * 128, (ch + 1) * 128)
        b_col = cbx[:, NT + ch:NT + ch + 1]
        ccol_b = bass.AP(cbx, ch, [[2 * NT, 128], [0, 128]])
        B[0].mm(slice(0, 128), ktT[:, csl], ktT[:, csl],
                deps=prep if ch == 0 else [ktn_done])
        gmm = kb.sig("t")
        B[1].mm(slice(0, 128), ones1f[0:1, :], c_row[0:1, csl], deps=[c_done])
        cbm = kb.sig("t")
        kb.stt("v", dm[:], B[1].h[:, 0:128], -1.0, ccol_b, ALU.mult, ALU.add,
               deps=[cbm, bt3])
        B[1].done(kb.sig("v"))
        dmt = kb.sig("v")
        kb.ts("v", scr[:], dm[:], 0.0, ALU.min, deps=[dmt])
        s1 = kb.sig("v")
        kb.act(gamt[:], scr[:], AF.Exp, deps=[s1])
        gam_s = kb.sig("s")
        if STEP < 1:
            continue
        kb.ts("v", scr[:], dm[:], -1.0, ALU.mult, s2=0.0, op1=ALU.min, deps=[gam_s])
        s2_ = kb.sig("v")
        kb.act(gamtT[:], scr[:], AF.Exp, deps=[s2_])
        gamT_s = kb.sig("s")
        if STEP < 2:
            continue
        kb.stt("v", Pt[:], B[0].h[:, 0:128], b_col, NEGSL[:], ALU.mult, ALU.mult,
               deps=[gmm, bt3])
        B[0].done(kb.sig("v"))
        Ptok = kb.sig("v")
        BT.tr(slice(0, 128), Pt[:], Ibf[:], deps=[Ptok])
        ptm = kb.sig("t")
        kb.cp("s", PTt[:], BT.h[:, 0:128], deps=[ptm])
        BT.done(kb.sig("s"))
        PTtok = kb.sig("s")
        if STEP < 3:
            continue
        kb.tt("v", Rt[:], Pt[:], Ibf[:], ALU.add, deps=[Ptok])
        Rtok = kb.sig("v")
        if pendA is not None:
            chA = emit_chainA(*pendA, chA)
            pendA = None
        curP, curPT, nxtP, nxtPT = Pt, PTt, P2t, PT2t
        curPtok, curPTtok = Ptok, PTtok
        for lev in range(NLEV):
            B[0].mm(slice(0, 128), curPT[:], curP[:], deps=[curPTtok, curPtok])
            sq1 = kb.sig("t")
            kb.cp("s", nxtP[:], B[0].h[:, 0:128], deps=[sq1])
            B[0].done(kb.sig("s"))
            nPtok = kb.sig("s")
            if lev < NLEV - 1:
                B[1].mm(slice(0, 128), curP[:], curPT[:],
                        deps=[curPtok, curPTtok])
                sq2 = kb.sig("t")
                kb.cp("s", nxtPT[:], B[1].h[:, 0:128], deps=[sq2])
                B[1].done(kb.sig("s"))
                nPTtok = kb.sig("s")
            else:
                BT.tr(slice(0, 128), nxtP[:], Ibf[:], deps=[nPtok])
                sqt = kb.sig("t")
                kb.cp("s", nxtPT[:], BT.h[:, 0:128], deps=[sqt])
                BT.done(kb.sig("s"))
                nPTtok = kb.sig("s")
            curP, curPT, nxtP, nxtPT = nxtP, nxtPT, curP, curPT
            curPtok, curPTtok = nPtok, nPTtok
            B[6].mm(slice(0, 128), curPT[:], Rt[:], deps=[curPTtok, Rtok])
            rum = kb.sig("t")
            kb.tt("v", Rt[:], B[6].h[:, 0:128], Rt[:], ALU.add, deps=[rum])
            B[6].done(kb.sig("v"))
            Rtok = kb.sig("v")
        if STEP < 4:
            continue
        BT.tr(slice(128, 256), Rt[:], Ibf[:], deps=[Rtok])
        rtm = kb.sig("t")
        kb.cp("s", RTt[:], BT.h[:, 128:256], deps=[rtm])
        BT.done(kb.sig("s"))
        RTtok = kb.sig("s")
        kb.tt("v", GRT[:], RTt[:], gamtT[:], ALU.mult, deps=[RTtok, gamT_s])
        GRTtok = kb.sig("v")
        if STEP < 5:
            continue
        BT.tr(slice(256, 384), ktT[:, csl], Ibf[:], deps=[ktn_done])
        ktm = kb.sig("t")
        kb.cp("s", KtA[:, csl], BT.h[:, 256:384], deps=[ktm])
        Kttok = kb.sig("s")
        kb.ts("v", Bk[:], BT.h[:, 256:384], b_col, ALU.mult,
              deps=[ktm, bt3, Kttok])
        Bktok = kb.sig("v")
        BT.tr(slice(384, 512), vT[:, csl], Ibf[:], deps=[ev[2], Bktok, Kttok])
        vtm = kb.sig("t")
        kb.ts("v", Bv[:], BT.h[:, 384:512], b_col, ALU.mult, deps=[vtm])
        Bvtok = kb.sig("v")
        BT.done(Bvtok)
        if STEP < 6:
            continue
        B[2].mm(slice(0, 128), Bk[:], RTt[:], deps=[Bktok, RTtok])
        wtm = kb.sig("t")
        kb.cp("s", WTA[:, csl], B[2].h[:, 0:128], deps=[wtm])
        B[2].done(kb.sig("s"))
        WTtok = kb.sig("s")
        B[3].mm(slice(0, 128), GRT[:], Bv[:], deps=[GRTtok, Bvtok])
        um = kb.sig("t")
        kb.ts("v", UeA[:, csl], B[3].h[:, 0:128], ecol[:, ch:ch + 1], ALU.mult,
              deps=[um, decay_s])
        B[3].done(kb.sig("v"))
        Uetok = kb.sig("v")
        B[4].mm(slice(0, 128), Bv[:], GRT[:], deps=[GRTtok, Bvtok])
        utm = kb.sig("t")
        kb.cp("s", UTt[:], B[4].h[:, 0:128], deps=[utm])
        B[4].done(kb.sig("s"))
        UTtok = kb.sig("s")
        if STEP < 7:
            continue
        B[5].mm(slice(0, 128), UTt[:], qT[:, csl], deps=[UTtok, ev[0]])
        t5m = kb.sig("t")
        kb.tt("v", scr[:], B[5].h[:, 0:128], gamtT[:], ALU.mult, deps=[t5m])
        B[5].done(kb.sig("v"))
        a21 = kb.sig("v")
        kb.tt("v", A2A[:, csl], scr[:], TRIU[:], ALU.mult, deps=[a21])
        A2tok = kb.sig("v")
        if STEP < 8:
            continue
        pendA = (ch, WTtok, Kttok, Uetok)

    if pendA is not None:
        chA = emit_chainA(*pendA, chA)
        pendA = None
    if NCH < NT:
        return finish_early()
    for e in range(14, 16):
        w2_toks[e] = kb.dma("w2", w2b[:, e * 2 * D:(e + 1) * 2 * D],
                            bass.AP(w2_h, e * FP * D, [[D, 128], [128 * D, 2], [1, D]]),
                            deps=chA)

    kb.ts("v", Zf32[:], Zc[:], hmcol[:, 0:1], ALU.mult, deps=chA + [hm_t])
    zf = kb.sig("v")
    ar1 = kb.dma("ccb", zin_d.ap(), Zf32[:], deps=[zf])
    cc1 = kb.collective("AllReduce", ALU.add, [[0, 1], [2, 3], [4, 5], [6, 7]],
                        zin_d, zout_d, deps=[ar1])
    ar2 = kb.dma("ccb", Zf32[:], zout_d.ap(), deps=[cc1])
    kb.ts("v", Zc[:], Zf32[:], epcol[:, 0:1], ALU.mult, deps=[ar2, ep_t])
    z0 = kb.sig("v")
    BT.tr(slice(0, 128), Zc[:], Ibf[:], deps=[z0])
    ztm = kb.sig("t")
    kb.cp("s", ZTc[:], BT.h[:, 0:128], deps=[ztm])
    BT.done(kb.sig("s"))
    chB = [kb.sig("s"), z0]
    TAPS.update(KtA=KtA[:], WTA=WTA[:], UeA=UeA[:], A2A=A2A[:], Zc=Zc[:],
                ZTc=ZTc[:], Zf32=Zf32[:])
    if PHASE <= 2:
        return finish_early()

    for ch in range(NT):
        csl = slice(ch * 128, (ch + 1) * 128)
        B[1].mm(slice(0, 128), KtA[:, csl], A2A[:, csl], deps=chB)
        iam = kb.sig("t")
        kb.cp("s", OA[:], B[1].h[:, 0:128], deps=[iam])
        B[1].done(kb.sig("s"))
        oat = kb.sig("s")
        B[2].mm(slice(0, 128), ZTc[:], qT[:, csl], start=True, stop=True,
                deps=chB)
        mqm = kb.sig("t")
        kb.cp("v", MqT[:], B[2].h[:, 0:128], deps=[mqm])
        mqt = kb.sig("v")
        B[3].mm(slice(0, 128), WTA[:, csl], MqT[:], deps=[mqt])
        t4m = kb.sig("t")
        kb.tt("v", T4n[:], B[3].h[:, 0:128], NTRIU[:], ALU.mult, deps=[t4m])
        B[3].done(kb.sig("v"))
        t4t = kb.sig("v")
        B[6].mm(slice(0, 128), KtA[:, csl], T4n[:], deps=[t4t])
        bbm = kb.sig("t")
        B[6].mm(slice(384, 512), ones1[0:1, 0:128],
                growRb[0:1, ch * 128:(ch + 1) * 128], deps=[growNT_done])
        gbm = kb.sig("t")
        kb.cp("s", gbc[:], B[6].h[:, 384:512], deps=[gbm])
        gbct = kb.sig("s")
        kb.tt("v", Ot1[:], B[6].h[:, 0:128], MqT[:], ALU.add, deps=[bbm, gbct])
        B[6].done(kb.sig("v"))
        B[2].done(mqt)
        kb.tt("v", Ot1[:], Ot1[:], gbc[:], ALU.mult)
        o1t = kb.sig("v")
        kb.tt("v", Ot1[:], Ot1[:], OA[:], ALU.add, deps=[oat, o1t])
        o2t = kb.sig("v")
        kb.tt("v", ogT[:, csl], Ot1[:], gT[:, csl], ALU.mult, deps=[o2t, ev[3]])
        kb.sig("v")
        kb.ts("v", IgL[:], Ibf[:], gLbc[:, ch:ch + 1], ALU.mult)
        igl = kb.sig("v")
        B[0].mm(slice(128, 256), WTA[:, csl], Zc[:], deps=chB)
        wzm = kb.sig("t")
        kb.ts("v", WZs[:], B[0].h[:, 128:256], gLbc[:, ch:ch + 1], ALU.mult,
              s2=-1.0, op1=ALU.mult, deps=[wzm])
        B[0].done(kb.sig("v"))
        wzs = kb.sig("v")
        B[4].mm(slice(128, 256), IgL[:], Zc[:], start=True, stop=False,
                deps=[igl] + chB)
        B[4].mm(slice(128, 256), KtA[:, csl], WZs[:], start=False, stop=False,
                deps=[wzs])
        B[4].mm(slice(128, 256), KtA[:, csl], UeA[:, csl], start=False, stop=True)
        znm = kb.sig("t")
        B[5].mm(slice(128, 256), IgL[:], ZTc[:], start=True, stop=False,
                deps=[igl] + chB)
        B[5].mm(slice(128, 256), WZs[:], KtA[:, csl], start=False, stop=False,
                deps=[wzs])
        B[5].mm(slice(128, 256), UeA[:, csl], KtA[:, csl], start=False, stop=True)
        ztnm = kb.sig("t")
        kb.cp("s", Zc[:], B[4].h[:, 128:256], deps=[znm])
        B[4].done(kb.sig("s"))
        kb.cp("s", ZTc[:], B[5].h[:, 128:256], deps=[ztnm])
        B[5].done(kb.sig("s"))
        chB = [kb.sig("s")]
    og_all = kb.sig("v")
    for e in range(8, 12):
        w2_toks[e] = kb.dma("w2", w2b[:, e * 2 * D:(e + 1) * 2 * D],
                            bass.AP(w2_h, e * FP * D, [[D, 128], [128 * D, 2], [1, D]]),
                            deps=chB + [og_all])

    if PHASE <= 3:
        return finish_early()
    x1_st = []
    xrow_rd = [None, None]
    zrow_rd = [None, None]
    x1tok_rd = [None, None]
    col_rd = [None, None]
    for ti in range(NT):
        tp = ti % 2
        pc = slice(tp, tp + 1)
        tsl = slice(ti * 128, (ti + 1) * 128)
        bk0, bk1 = (B[0], B[1]) if tp == 0 else (B[2], B[3])
        xr = kb.dma("xrow", xrow_t[:, :], xrows_h.ap()[tsl, :],
                    deps=[xrow_rd[0]] if xrow_rd[0] else ())
        for dh, bk in ((0, bk0), (1, bk1)):
            bk.mm(S512, ogT[:, tsl], wob[:, dh * 512:(dh + 1) * 512],
                  start=True, stop=False, deps=[og_all, wo_t] + chB)
            bk.mm(S512, ones1[0:1, 0:128], borow[0:1, dh * 512:(dh + 1) * 512],
                  start=False, stop=True, deps=[bo_t])
        ymm = kb.sig("t")
        zdeps = [ymm, xr] + ([zrow_rd[tp]] if zrow_rd[tp] else [])
        kb.tt("v", zrow[:, tp * D:tp * D + 512], bk0.h[:, S512],
              xrow_t[:, 0:512], ALU.add, deps=zdeps)
        kb.tt("v", zrow[:, tp * D + 512:(tp + 1) * D], bk1.h[:, S512],
              xrow_t[:, 512:1024], ALU.add, deps=[ymm, xr])
        ztok = kb.sig("v")
        zsl = zrow[:, tp * D:(tp + 1) * D]
        xrow_rd[0] = ztok
        bk0.done(ztok)
        bk1.done(ztok)
        cdeps = [ztok] + ([col_rd[tp]] if col_rd[tp] else [])
        kb.reduce(sumc[:, pc], zsl, ALU.add, deps=cdeps)
        s1 = kb.sig("v")
        kb.act(outf[:, :], zsl, AF.Square, accum=sqc[:, pc], deps=cdeps)
        s2_ = kb.sig("s")
        kb.ts("v", mucol[:, pc], sumc[:, pc], 1.0 / D, ALU.mult, deps=[s1])
        kb.ts("v", varcol[:, pc], sqc[:, pc], 1.0 / D, ALU.mult, deps=[s2_])
        vready = kb.sig("v")
        kb.stt("v", varcol[:, pc], mucol[:, pc], mucol[:, pc], varcol[:, pc],
               ALU.mult, ALU.subtract, deps=[vready])
        nv = kb.sig("v")
        kb.act(rstdc[:, pc], varcol[:, pc], AF.Sqrt, scale=-1.0, bias=LNEPS,
               deps=[nv])
        sr = kb.sig("s")
        kb.raw("v", lambda v, _pc=pc: v.reciprocal(out=rstdc[:, _pc],
                                                   in_=rstdc[:, _pc]), deps=[sr])
        rst = kb.sig("v")
        x1deps = [rst] + ([x1tok_rd[0]] if x1tok_rd[0] else [])
        kb.ts("v", x1tok[:, :], zsl, mucol[:, pc],
              ALU.subtract, s2=rstdc[:, pc], op1=ALU.mult, deps=x1deps)
        x1t = kb.sig("v")
        col_rd[tp] = x1t
        zrow_rd[tp] = x1t
        x1_st.append(kb.dma("x1st", x1_d.ap()[tsl, :],
                            x1tok[:, :], deps=[x1t]))
        x1tok_rd[0] = x1_st[-1]

    if PHASE <= 4:
        return finish_early()
    x1T_rd = [None, None]

    def x1T_load(tc):
        buf = tc % 2
        deps = [x1_st[4 * tc + 3]]
        if x1T_rd[buf] is not None:
            deps = deps + [x1T_rd[buf]]
        toks = []
        for dc in range(DC):
            dst = x1T[:, buf * DC * 512 + dc * 512:buf * DC * 512 + (dc + 1) * 512]
            src = x1_d.ap()[tc * 512:(tc + 1) * 512, dc * 128:(dc + 1) * 128]
            toks.append(kb.dma("x1t%d" % buf, dst, src,
                               deps=deps if dc == 0 else (), transpose=True))
        return toks

    for tc in range(4):
        buf = tc % 2
        ltoks = x1T_load(tc)
        last_rm = None
        for t4 in range(4):
            ti = tc * 4 + t4
            tp = ti % 2
            bk = B[6] if tp == 0 else B[5]
            psl = slice(0, E)
            bk.mm(psl, ones1[0:1, 0:128], bgrow[0:1, :], start=True, stop=False,
                  deps=(ltoks + [wgg_t, bgr_t]) if t4 == 0 else ())
            for dc in range(DC):
                bk.mm(psl,
                      x1T[:, buf * DC * 512 + dc * 512 + t4 * 128:
                          buf * DC * 512 + dc * 512 + (t4 + 1) * 128],
                      wg16b[:, dc * E:(dc + 1) * E],
                      start=False, stop=(dc == DC - 1))
            rm = kb.sig("t")
            last_rm = rm
            kb.cp("s", sctok[:, tp * E:(tp + 1) * E], bk.h[:, 0:E], deps=[rm])
            bk.done(kb.sig("s"))
            sck = kb.sig("s")
            kb.raw("v", lambda v, _tp=tp: v.max(
                out=m2col[:, _tp * 8:(_tp + 1) * 8],
                in_=sctok[:, _tp * E:(_tp + 1) * E]), deps=[sck])
            m8 = kb.sig("v")
            kb.ts("v", msk_tok[:, tp * 128:tp * 128 + E],
                  sctok[:, tp * E:(tp + 1) * E],
                  m2col[:, tp * 8 + 1:tp * 8 + 2], ALU.is_ge, deps=[m8])
            mskt = kb.sig("v")
            BT.tr(slice(640 + tp * 128, 768 + tp * 128),
                  msk_tok[:, tp * 128:(tp + 1) * 128], Ibf[:],
                  deps=[mskt, g_init])
            mtm = kb.sig("t")
            kb.cp("s", maskT[0:E, ti * 128:(ti + 1) * 128],
                  BT.h[0:E, 640 + tp * 128:640 + tp * 128 + 128], deps=[mtm])
            BT.done(kb.sig("s"))
        x1T_rd[buf] = last_rm
    mask_done = kb.sig("s")
    kb.reduce(nposl[0:E, 0:1], maskT[0:E, :], ALU.add, deps=[mask_done])
    npt = kb.sig("v")
    ag1 = kb.dma("ccb", cin_d.ap(), nposl[0:E, 0:1], deps=[npt])
    cc2 = kb.collective("AllGather", ALU.bypass, [[0, 1, 2, 3, 4, 5, 6, 7]],
                        cin_d, cout_d, deps=[ag1])
    ag2 = kb.dma("ccb", cnt8[:], cout_d.ap(), deps=[cc2])
    kb.ts("v", nzl[:], cnt8[:], -1.0, ALU.mult, s2=float(TOK), op1=ALU.add,
          deps=[ag2])
    nzt = kb.sig("v")
    B[6].mm(slice(0, E), bm8c[0:8, 0:1], cnt8[0:8, 0:E], deps=[ag2, bmm_t],
            rows=slice(0, 1))
    B[6].mm(slice(16, 16 + E), pm8c[0:8, 0:1], nzl[0:8, 0:E], deps=[nzt, pm_t],
            rows=slice(0, 1))
    cmt = kb.sig("t")
    kb.cp("v", cntrow[0:1, :], B[6].h[0:1, 0:32], deps=[cmt])
    B[6].done(kb.sig("v"))
    crt = kb.sig("v")
    cb1 = kb.dma("misc", crd.ap(), cntrow[0:1, :], deps=[crt])
    cb2 = kb.dma("misc", padc[0:E, 0:1], bass.AP(crd, 0, [[1, E], [1, 1]]),
                 deps=[cb1])
    cb3 = kb.dma("misc", offsc[0:E, 0:1], bass.AP(crd, 16, [[1, E], [1, 1]]),
                 deps=[cb1])
    kb.ts("v", padc[0:E, 0:1], padc[0:E, 0:1], -1.0, ALU.mult, s2=float(TOK),
          op1=ALU.add, deps=[cb2])
    padt = kb.sig("v")
    kb.raw("g", lambda g: g.memset(tmp16[0:E, :], 0.0))
    z16t = kb.sig("g")
    kb.ts("v", zT16[0:E, :], maskT[0:E, :], -1.0, ALU.mult, s2=1.0, op1=ALU.add,
          deps=[mask_done])
    ztt = kb.sig("v")
    kb.raw("v", lambda v: v.tensor_tensor_scan(
        out=zT16[0:E, :], data0=zT16[0:E, :], data1=tmp16[0:E, :], initial=0.0,
        op0=ALU.add, op1=ALU.add), deps=[ztt, z16t])
    zst = kb.sig("v")
    kb.ts("v", zT16[0:E, :], zT16[0:E, :], offsc[0:E, 0:1], ALU.add,
          s2=padc[0:E, 0:1], op1=ALU.is_le, deps=[zst, padt, cb3])
    condt = kb.sig("v")
    kb.ts("v", tmp16[0:E, :], maskT[0:E, :], -1.0, ALU.mult, s2=1.0, op1=ALU.add,
          deps=[mask_done, zst])
    tmt = kb.sig("v")
    kb.tt("v", tmp16[0:E, :], tmp16[0:E, :], zT16[0:E, :], ALU.mult,
          deps=[tmt, condt])
    tzt = kb.sig("v")
    kb.tt("v", selT[0:E, :], maskT[0:E, :], tmp16[0:E, :], ALU.add, deps=[tzt])
    sel_done = kb.sig("v")

    if PHASE <= 5:
        return finish_early()
    B[6].mm(S512, ones1[0:1, 0:128], grow_[0:1, 0:512], deps=[gr_t, sel_done])
    gm1 = kb.sig("t")
    kb.cp("s", gamb[:, 0:512], B[6].h[:, S512], deps=[gm1])
    B[6].done(kb.sig("s"))
    B[6].mm(S512, ones1[0:1, 0:128], grow_[0:1, 512:1024], deps=[gr_t])
    gm2 = kb.sig("t")
    kb.cp("s", gamb[:, 512:1024], B[6].h[:, S512], deps=[gm2])
    B[6].done(kb.sig("s"))
    B[6].mm(S512, ones1[0:1, 0:128], brow_[0:1, 0:512], deps=[br_t])
    bm1 = kb.sig("t")
    kb.cp("s", betb[:, 0:512], B[6].h[:, S512], deps=[bm1])
    B[6].done(kb.sig("s"))
    B[6].mm(S512, ones1[0:1, 0:128], brow_[0:1, 512:1024], deps=[br_t])
    bm2 = kb.sig("t")
    kb.cp("s", betb[:, 512:1024], B[6].h[:, S512], deps=[bm2])
    B[6].done(kb.sig("s"))
    gb_done = kb.sig("s")

    if debug_taps:
        dbg_sel = nc.dram_tensor("dbg_selT", [E, TOK], BF16, kind="ExternalOutput")
        dsl = kb.dma("misc", dbg_sel.ap(), selT[0:E, :], deps=[sel_done])
        dbg_x1 = nc.dram_tensor("dbg_x1", [TOK, D], BF16, kind="ExternalOutput")
        dx1 = kb.dma("misc", dbg_x1.ap(), x1_d.ap(), deps=x1_st)
        dbg_og = nc.dram_tensor("dbg_ogT", [128, TOK], BF16, kind="ExternalOutput")
        dog = kb.dma("misc", dbg_og.ap(), ogT[:], deps=[og_all])
        kb.wait("y", dsl)
        kb.wait("y", dx1)
        kb.wait("y", dog)

    out_toks = []
    hb_ready = [sel_done]
    x1ld_rd = [None, None]
    zrow_rd2 = [None, None]
    col_rd2 = [None, None]
    outf_rd = [None, None]
    w1_ld = {}

    def w1_issue(i, deps):
        tc4, e = divmod(i, E)
        wbuf = i % 3
        w1_ld[i] = kb.dma("w1s%d" % (wbuf % 2),
                          w1cur[:, wbuf * 2048:(wbuf + 1) * 2048],
                          w1bf_d.ap()[e], deps=list(deps) + [w1pre[e]])

    for _i in range(3):
        w1_issue(_i, [])
    w1_mm_done = {}
    pending_ltoks = {0: None}
    for tc in range(4):
        buf = tc % 2
        tcs = slice(tc * 512, (tc + 1) * 512)
        ltoks = pending_ltoks.get(tc) or x1T_load(tc)
        h_toks = []
        for e in range(E):
            i = tc * E + e
            wbuf = i % 3
            ep = e % 2
            bks = (B[0], B[1]) if ep == 0 else (B[2], B[3])
            selbk = B[6] if ep == 0 else B[5]
            selbk.mm(S512, IdE[0:E, e * 128:(e + 1) * 128], selT[0:E, tcs],
                     deps=[sel_done, msk_done] + (hb_ready if e < 2 else []))
            sbm = kb.sig("t")
            wdeps = [w1_ld[i]]
            for half in range(2):
                bk = bks[half]
                for dc in range(DC):
                    bk.mm(S512,
                          w1cur[:, wbuf * 2048 + dc * FP + half * 128:
                                wbuf * 2048 + dc * FP + half * 128 + 128],
                          x1T[:, buf * DC * 512 + dc * 512:
                              buf * DC * 512 + (dc + 1) * 512],
                          start=(dc == 0), stop=(dc == DC - 1),
                          deps=(ltoks + wdeps + hb_ready) if dc == 0 else ())
            m1 = kb.sig("t")
            w1_mm_done[(tc, e)] = m1
            if i + 3 < 64:
                w1_issue(i + 3, [m1])
            for half in range(2):
                kb.act(hbuf[:, (e * 2 + half) * 512:(e * 2 + half + 1) * 512],
                       bks[half].h[:, S512], AF.Gelu,
                       bias=b1c[:, e * 2 + half:e * 2 + half + 1],
                       deps=[m1, b1_t] + hb_ready)
            gl = kb.sig("s")
            bks[0].done(gl)
            bks[1].done(gl)
            for half in range(2):
                kb.tt("v", hbuf[:, (e * 2 + half) * 512:(e * 2 + half + 1) * 512],
                      hbuf[:, (e * 2 + half) * 512:(e * 2 + half + 1) * 512],
                      selbk.h[:, S512], ALU.mult, deps=[sbm, gl])
            selbk.done(kb.sig("v"))
            h_toks.append(kb.sig("v"))
        x1T_rd[buf] = w1_mm_done[(tc, E - 1)]
        if tc + 1 < 4:
            pending_ltoks[tc + 1] = x1T_load(tc + 1)
        stg_sigs = []
        for pair in range(2):
            tis = (2 * pair, 2 * pair + 1)
            bks = {(ti4, dh): B[2 + 2 * (ti4 - tis[0]) + dh]
                   for ti4 in tis for dh in range(2)}
            for ti4 in tis:
                gtile = tc * 4 + ti4
                for dh in range(2):
                    bks[(ti4, dh)].mm(
                        S512, selT[0:E, gtile * 128:(gtile + 1) * 128],
                        b2r[0:E, dh * 512:(dh + 1) * 512], start=True,
                        stop=False, deps=[sel_done, b2_t] + h_toks)
            for e in range(E):
                for fpc in range(2):
                    for ti4 in tis:
                        for dh in range(2):
                            bks[(ti4, dh)].mm(
                                S512,
                                hbuf[:, (e * 2 + fpc) * 512 + ti4 * 128:
                                     (e * 2 + fpc) * 512 + (ti4 + 1) * 128],
                                w2b[:, e * 2 * D + fpc * D + dh * 512:
                                    e * 2 * D + fpc * D + (dh + 1) * 512],
                                start=False, stop=(e == E - 1 and fpc == 1),
                                deps=[w2_toks[e]]
                                if (e == 0 and fpc == 0 and ti4 == tis[0]
                                    and dh == 0) else ())
            m2m = kb.sig("t")
            for ti4 in tis:
                for dh in range(2):
                    kb.cp("s", ostg[:, ti4 * D + dh * 512:ti4 * D + (dh + 1) * 512],
                          bks[(ti4, dh)].h[:, S512], deps=[m2m])
                    sg = kb.sig("s")
                    bks[(ti4, dh)].done(sg)
                    stg_sigs.append(sg)
        hb_ready = stg_sigs
        for ti4 in range(4):
            gtile = tc * 4 + ti4
            tp = gtile % 2
            pc = slice(tp, tp + 1)
            xl = kb.dma("x1ld", x1ld[:, :],
                        x1_d.ap()[gtile * 128:(gtile + 1) * 128, :],
                        deps=[x1_st[gtile]] + ([x1ld_rd[0]] if x1ld_rd[0] else []))
            osl = ostg[:, ti4 * D:(ti4 + 1) * D]
            zsl = zrow[:, tp * D:(tp + 1) * D]
            zdeps = stg_sigs + [xl] + ([zrow_rd2[tp]] if zrow_rd2[tp] else [])
            kb.tt("g", zsl, osl, x1ld[:, :], ALU.add,
                  deps=zdeps)
            z2t = kb.sig("g")
            cdeps = [z2t] + ([col_rd2[tp]] if col_rd2[tp] else [])
            kb.reduce(sumc[:, pc], zsl, ALU.add, deps=cdeps)
            s1 = kb.sig("v")
            kb.act(outf[:, :], zsl, AF.Square, accum=sqc[:, pc],
                   deps=cdeps + ([outf_rd[0]] if outf_rd[0] else []))
            s2_ = kb.sig("s")
            kb.ts("v", mucol[:, pc], sumc[:, pc], 1.0 / D, ALU.mult, deps=[s1])
            kb.ts("v", varcol[:, pc], sqc[:, pc], 1.0 / D, ALU.mult, deps=[s2_])
            vready = kb.sig("v")
            kb.stt("v", varcol[:, pc], mucol[:, pc], mucol[:, pc],
                   varcol[:, pc], ALU.mult, ALU.subtract, deps=[vready])
            nv = kb.sig("v")
            kb.act(rstdc[:, pc], varcol[:, pc], AF.Sqrt, scale=-1.0, bias=LNEPS,
                   deps=[nv])
            sr = kb.sig("s")
            kb.raw("v", lambda v, _pc=pc: v.reciprocal(out=rstdc[:, _pc],
                                                       in_=rstdc[:, _pc]),
                   deps=[sr])
            rs = kb.sig("v")
            t2sl = t2b[:, tp * D:(tp + 1) * D]
            kb.ts("v", t2sl, zsl, mucol[:, pc], ALU.subtract,
                  s2=rstdc[:, pc], op1=ALU.mult, deps=[rs])
            t2t = kb.sig("v")
            kb.tt("v", t2sl, t2sl, gamb[:, :], ALU.mult, deps=[t2t, gb_done])
            t3t = kb.sig("v")
            kb.tt("g", t2sl, t2sl, betb[:, :], ALU.add, deps=[t3t])
            t4t = kb.sig("g")
            z3sl = t2sl
            kb.tt("g", z3sl, t2sl, x1ld[:, :], ALU.add,
                  deps=[t4t])
            z3t = kb.sig("g")
            x1ld_rd[0] = z3t
            zrow_rd2[tp] = z3t
            cdeps2 = [z3t]
            kb.reduce(sumc[:, pc], z3sl, ALU.add, deps=cdeps2)
            s3 = kb.sig("v")
            kb.act(outf[:, :], z3sl, AF.Square, accum=sqc[:, pc], deps=cdeps2)
            s4 = kb.sig("s")
            kb.ts("v", mucol[:, pc], sumc[:, pc], 1.0 / D, ALU.mult, deps=[s3])
            kb.ts("v", varcol[:, pc], sqc[:, pc], 1.0 / D, ALU.mult, deps=[s4])
            vready2 = kb.sig("v")
            kb.stt("v", varcol[:, pc], mucol[:, pc], mucol[:, pc],
                   varcol[:, pc], ALU.mult, ALU.subtract, deps=[vready2])
            nv2 = kb.sig("v")
            kb.act(rstdc[:, pc], varcol[:, pc], AF.Sqrt, scale=-1.0,
                   bias=LNEPS, deps=[nv2])
            sr2 = kb.sig("s")
            kb.raw("v", lambda v, _pc=pc: v.reciprocal(out=rstdc[:, _pc],
                                                       in_=rstdc[:, _pc]),
                   deps=[sr2])
            rs2 = kb.sig("v")
            ofsl = outf[:, :]
            kb.ts("v", ofsl, z3sl, mucol[:, pc], ALU.subtract,
                  s2=rstdc[:, pc], op1=ALU.mult,
                  deps=[rs2] + ([outf_rd[0]] if outf_rd[0] else []))
            oft = kb.sig("v")
            col_rd2[tp] = oft
            out_toks.append(kb.dma("out",
                                   out_h.ap()[gtile * 128:(gtile + 1) * 128, :],
                                   ofsl, deps=[oft]))
            outf_rd[0] = out_toks[-1]

    for tok in out_toks:
        kb.wait("y", tok)
        kb.wait("g", tok)

    return kb.build()


_CACHE = {}


def kernel(**inputs):
    if "nc" not in _CACHE:
        _CACHE["nc"] = build_full(debug_taps=bool(os.environ.get("KNL_TAPS")))
    nc = _CACHE["nc"]
    in_maps = make_in_maps(inputs)
    res = run_bass_kernel_spmd(nc, in_maps, core_ids=list(range(8)))
    shards = [np.asarray(res.results[r]["out"], np.float32) for r in range(8)]
    out = np.stack(shards, 0).reshape(4, 4096, D)
    if os.environ.get("KNL_TAPS"):
        kernel.last_results = res
    return out


def make_in_maps(inputs):
    f32 = lambda a: np.ascontiguousarray(np.asarray(a, np.float32))
    x = f32(inputs["L0_attn_x"])
    Wa = f32(inputs["L0_attn_Wa"]); Wb = f32(inputs["L0_attn_Wb"])
    ba = f32(inputs["L0_attn_ba"]); bb = f32(inputs["L0_attn_bb"])
    b1 = f32(inputs["L0_ffn_b1"])
    shared = dict(
        Wq=f32(inputs["L0_attn_Wq"]), Wk=f32(inputs["L0_attn_Wk"]),
        Wv=f32(inputs["L0_attn_Wv"]), Wg=f32(inputs["L0_attn_Wg"]),
        Wab=np.ascontiguousarray(np.concatenate([Wa, Wb], 1)),
        Wo=f32(inputs["L0_attn_Wo"]),
        bqkvg=np.ascontiguousarray(np.stack(
            [f32(inputs["L0_attn_bq"]), f32(inputs["L0_attn_bk"]),
             f32(inputs["L0_attn_bv"]), f32(inputs["L0_attn_bg"])], 1)),
        bab=np.array([[ba[0]], [bb[0]]], np.float32),
        bo_row=np.ascontiguousarray(f32(inputs["L0_attn_bo"])[None, :]),
        wg16=f32(inputs["L0_ffn_wg"]),
        bg16=np.ascontiguousarray(f32(inputs["L0_ffn_bg"]).reshape(E, 1)),
        w1=f32(inputs["L0_ffn_w1"]),
        b1c=np.ascontiguousarray(
            b1.reshape(E, 2, 128).transpose(2, 0, 1).reshape(128, 32)),
        w2=f32(inputs["L0_ffn_w2"]),
        b2r=f32(inputs["L0_ffn_b2"]),
        gamma_row=np.ascontiguousarray(f32(inputs["L0_ffn_gamma"])[None, :]),
        beta_row=np.ascontiguousarray(f32(inputs["L0_ffn_beta"])[None, :]),
    )
    maps = []
    for r in range(8):
        b, h = r // 2, r % 2
        sl = slice(h * TOK, (h + 1) * TOK)
        m = dict(shared)
        m["xT"] = np.ascontiguousarray(x[b].T[:, sl])
        m["xrows"] = np.ascontiguousarray(x[b, sl])
        m["hmask"] = np.full((128, 1), 1.0 if h == 0 else 0.0, np.float32)
        m["eps_role"] = np.full((128, 1), 1.0 if h == 1 else 0.0, np.float32)
        pm = np.zeros((8, 1), np.float32)
        if h == 1:
            pm[2 * b, 0] = 1.0
        m["pm8"] = pm
        bm = np.zeros((8, 1), np.float32)
        bm[2 * b, 0] = 1.0
        bm[2 * b + 1, 0] = 1.0
        m["bm8"] = bm
        maps.append(m)
    return maps
